# version 33
# baseline (speedup 1.0000x reference)
import contextlib
import numpy as np
import ml_dtypes
import concourse.bass as bass
import concourse.mybir as mybir
from concourse.bass_utils import run_bass_kernel_spmd

F32 = mybir.dt.float32
BF16 = mybir.dt.bfloat16
I32 = mybir.dt.int32
AF = mybir.ActivationFunctionType
OP = mybir.AluOpType
AX = mybir.AxisListType
NPBF = ml_dtypes.bfloat16

B, L, DM = 4, 8192, 1024
DI, NST, DTR, CONV = 2048, 16, 64, 4
IFF = 2816
TLOC = 4096
NCORE = 8
TAU = 512
TC = 1024
NJU = DI // 128
NJI = 2 * DI // 128
NJB = 8
NKD = DM // 128
PAIRS = [[0, 1], [2, 3], [4, 5], [6, 7]]
ALLG = [list(range(8))]
MAGIC = 6291456.0

ENGS = ("sync", "vector", "scalar", "tensor", "gpsimd")
ESEM = {"sync": "ld", "vector": "dve", "scalar": "act", "tensor": "pe",
        "gpsimd": "gp"}
DEBUG_TAPS = False
BUILD_PHASES = "ALL"


_SKIP_REST = [False]


class Prog:

    NDQ = 8

    def __init__(self, nc, es):
        self.nc = nc
        self.es = es
        self.sems = {}
        self.counts = {}
        self.ops = {e: [] for e in ENGS}
        self.last_w = {}
        self.readers = {}
        self.hwm = {e: {} for e in ENGS}
        self.fam = {}
        self.fam_next = 0

    def sem(self, name):
        if name not in self.sems:
            self.sems[name] = self.es.enter_context(self.nc.semaphore(name))
            self.counts.setdefault(name, 0)
        return self.sems[name]

    def val(self, name):
        return self.counts.get(name, 0)

    def _family(self, buf):
        if buf not in self.fam:
            self.fam[buf] = self.fam_next % self.NDQ
            self.fam_next += 1
        return self.fam[buf]

    def op(self, eng, fn, reads=(), writes=(), imm_reads=(), dma=False,
           extra_waits=(), sem_override=None, inc_n=None):
        if sem_override is not None:
            semn = sem_override
        elif dma:
            key = writes[0] if writes else (reads[0] if reads else "misc")
            semn = f"dq{self._family(key)}"
        else:
            semn = ESEM[eng]
        self.sem(semn)
        n = inc_n if inc_n is not None else (16 if dma else 1)

        deps = {}

        def add(tok, force=False):
            if tok is None:
                return
            kind, s, v, peng = tok
            if kind == "dma" or kind == "cc":
                v = self.counts[s]
            elif not force and not dma and peng == eng == "tensor":
                return
            if deps.get(s, -1) < v:
                deps[s] = v

        for b in reads:
            add(self.last_w.get(b))
        for b in imm_reads:
            add(self.last_w.get(b), force=True)
        for b in writes:
            add(self.last_w.get(b))
            for tok in self.readers.get(b, ()):
                add(tok)
        for s, v in extra_waits:
            self.sem(s)
            if deps.get(s, -1) < v:
                deps[s] = v

        waits = []
        for s, v in deps.items():
            if self.hwm[eng].get(s, -1) >= v:
                continue
            self.hwm[eng][s] = v
            waits.append((s, v))

        self.counts[semn] += n
        kind = "dma" if dma else ("cc" if sem_override and
                                  sem_override.startswith("cc") else "eng")
        tok = (kind, semn, self.counts[semn], eng)
        for b in writes:
            self.last_w[b] = tok
            self.readers[b] = []
        for b in list(reads) + list(imm_reads):
            self.readers.setdefault(b, []).append(tok)
        self.ops[eng].append((waits, fn, self.sems[semn], n))
        return tok

    def drain_all(self):
        snap = [(s, v) for s, v in self.counts.items()
                if not s.startswith("cc")]
        self.op("gpsimd", lambda g: None, extra_waits=snap, inc_n=1)

    def fence(self):
        snap = [(s, v) for s, v in self.counts.items()
                if not s.startswith("cc")]
        for e in ENGS:
            self.op(e, lambda _e: None, extra_waits=snap, inc_n=1)

    def dma(self, eng, out, in_, reads=(), writes=(), **kw):
        def f(e, out=out, in_=in_):
            return e.dma_start(out=out, in_=in_)
        return self.op(eng, f, reads=reads, writes=writes, dma=True, **kw)

    def flush(self):
        ops = self.ops
        self.ops = {e: [] for e in ENGS}
        with self.nc.Block() as block:
            for ename in ENGS:
                if not ops[ename]:
                    continue

                def body(eng, _ops=ops[ename]):
                    for waits, fn, inc_sem, inc_n in _ops:
                        for s, v in waits:
                            eng.wait_ge(self.sems[s], v)
                        instr = fn(eng)
                        if instr is None:
                            eng.sem_inc(inc_sem, inc_n)
                        else:
                            instr.then_inc(inc_sem, inc_n)
                getattr(block, ename)(body)


def emit_quant(p, src, dst_bf, pdim, fdim, scr, src_buf, dst_buf,
               extra_waits=(), pfx="q", mcols=None, bs_pfx=None):
    nb = fdim // 16
    vv, av, t1, b2, s16 = scr["vv"], scr["av"], scr["t1"], scr["b2"], scr["s16"]
    sg = b2
    bm, sc = scr["bm"], scr["sc"]
    bs = bs_pfx or pfx

    p.op("vector", lambda v: v.tensor_reduce(
        out=bm[0:pdim, 0:nb], in_=src.rearrange("p (b s) -> p b s", s=16),
        axis=AX.X, op=OP.max, apply_absolute_value=True),
        reads=[src_buf], writes=[bs + "_bm"], extra_waits=extra_waits)
    p.op("vector", lambda v: v.tensor_scalar(
        out=sc[0:pdim, 0:nb], in0=bm[0:pdim, 0:nb], scalar1=1.0 / 6.0,
        scalar2=1e-12, op0=OP.mult, op1=OP.max),
        reads=[bs + "_bm"], writes=[bs + "_sc"])
    p.op("vector", lambda v: v.reciprocal(out=bm[0:pdim, 0:nb],
                                          in_=sc[0:pdim, 0:nb]),
         reads=[bs + "_sc"], writes=[bs + "_bm"])
    p.op("vector", lambda v: v.tensor_copy(
        out=s16[0:pdim, 0:fdim].rearrange("p (b s) -> p b s", s=16),
        in_=sc[0:pdim, 0:nb].rearrange("p (b o) -> p b o", o=1).broadcast_to(
            [pdim, nb, 16])), reads=[bs + "_sc"], writes=[bs + "_s16"])
    p.op("vector", lambda v: v.tensor_copy(
        out=vv[0:pdim, 0:fdim].rearrange("p (b s) -> p b s", s=16),
        in_=bm[0:pdim, 0:nb].rearrange("p (b o) -> p b o", o=1).broadcast_to(
            [pdim, nb, 16])), reads=[bs + "_bm"], writes=[pfx + "_vv"])
    p.op("vector", lambda v: v.tensor_tensor(
        out=vv[0:pdim, 0:fdim], in0=src, in1=vv[0:pdim, 0:fdim], op=OP.mult),
        reads=[src_buf, pfx + "_vv"], writes=[pfx + "_vv"])
    avf = av[0:pdim, 0:fdim]
    avi = av.ap().bitcast(I32)[0:pdim, 0:fdim]
    b2i = b2.ap().bitcast(I32)[0:pdim, 0:fdim]
    p.op("scalar", lambda s: s.activation(
        out=avf, in_=vv[0:pdim, 0:fdim], func=AF.Abs),
        reads=[pfx + "_vv"], writes=[pfx + "_av"])

    mp, mn = mcols
    p.op("scalar", lambda s: s.activation(
        out=t1[0:pdim, 0:fdim], in_=avf, func=AF.Identity,
        bias=mp[0:pdim, 0:1]),
        reads=[pfx + "_av"], imm_reads=["mcol"], writes=[pfx + "_t1"])
    p.op("scalar", lambda s: s.activation(
        out=t1[0:pdim, 0:fdim], in_=t1[0:pdim, 0:fdim], func=AF.Identity,
        bias=mn[0:pdim, 0:1]), reads=[pfx + "_t1"], writes=[pfx + "_t1"])
    p.op("vector", lambda v: v.tensor_scalar(
        out=b2i, in0=avi, scalar1=0x3FFFFF, scalar2=None, op0=OP.bitwise_and),
        reads=[pfx + "_av"], writes=[pfx + "_b2"])
    p.op("vector", lambda v: v.tensor_scalar(
        out=b2i, in0=b2i, scalar1=0x200000, scalar2=None, op0=OP.is_gt),
        reads=[pfx + "_b2"], writes=[pfx + "_b2"])
    p.op("vector", lambda v: v.tensor_scalar(
        out=avi, in0=avi, scalar1=22, scalar2=None, op0=OP.arith_shift_right),
        reads=[pfx + "_av"], writes=[pfx + "_av"])
    p.op("vector", lambda v: v.tensor_tensor(
        out=avi, in0=avi, in1=b2i, op=OP.add),
        reads=[pfx + "_av", pfx + "_b2"], writes=[pfx + "_av"])
    p.op("vector", lambda v: v.tensor_scalar(
        out=avi, in0=avi, scalar1=22, scalar2=None,
        op0=OP.logical_shift_left), reads=[pfx + "_av"], writes=[pfx + "_av"])
    p.op("vector", lambda v: v.tensor_scalar(
        out=avf, in0=avf, scalar1=1.0, scalar2=None, op0=OP.max),
        reads=[pfx + "_av"], writes=[pfx + "_av"])
    p.op("scalar", lambda s: s.activation(
        out=sg[0:pdim, 0:fdim], in_=vv[0:pdim, 0:fdim], func=AF.Sign),
        reads=[pfx + "_vv"], writes=[pfx + "_b2"])
    p.op("vector", lambda v: v.scalar_tensor_tensor(
        out=t1[0:pdim, 0:fdim], in0=t1[0:pdim, 0:fdim], scalar=1.0, in1=avf,
        op0=OP.min, op1=OP.add), reads=[pfx + "_t1", pfx + "_av"], writes=[pfx + "_t1"])
    p.op("vector", lambda v: v.scalar_tensor_tensor(
        out=t1[0:pdim, 0:fdim], in0=t1[0:pdim, 0:fdim], scalar=-1.0,
        in1=s16[0:pdim, 0:fdim], op0=OP.add, op1=OP.mult),
        reads=[pfx + "_t1", bs + "_s16"], writes=[pfx + "_t1"])
    p.op("vector", lambda v: v.tensor_tensor(
        out=dst_bf, in0=t1[0:pdim, 0:fdim], in1=sg[0:pdim, 0:fdim],
        op=OP.mult), reads=[pfx + "_t1", pfx + "_b2"], writes=[dst_buf])


def emit_rsqrt(p, dst, dst_buf, var_ap, var_buf, eps, sa, sb, pdim):
    di = dst.ap().bitcast(I32)[0:pdim, 0:1]
    ai = sa.ap().bitcast(I32)[0:pdim, 0:1]
    p.op("vector", lambda v: v.tensor_scalar(
        out=sa[0:pdim, 0:1], in0=var_ap, scalar1=eps, scalar2=None,
        op0=OP.add), reads=[var_buf], writes=["rs_a"])
    p.op("vector", lambda v: v.tensor_scalar(
        out=di, in0=ai, scalar1=1, scalar2=None, op0=OP.arith_shift_right),
        reads=["rs_a"], writes=["rs_d"])
    p.op("vector", lambda v: v.tensor_scalar(
        out=di, in0=di, scalar1=0x5F370000, scalar2=None, op0=OP.subtract),
        reads=["rs_d"], writes=["rs_d"])
    p.op("vector", lambda v: v.tensor_scalar(
        out=di, in0=di, scalar1=0, scalar2=None, op0=OP.bitwise_not),
        reads=["rs_d"], writes=["rs_d"])
    p.op("vector", lambda v: v.tensor_scalar(
        out=di, in0=di, scalar1=1, scalar2=None, op0=OP.add),
        reads=["rs_d"], writes=["rs_d"])
    for _ in range(3):
        p.op("vector", lambda v: v.tensor_tensor(
            out=sb[0:pdim, 0:1], in0=dst[0:pdim, 0:1], in1=dst[0:pdim, 0:1],
            op=OP.mult), reads=["rs_d"], writes=["rs_b"])
        p.op("vector", lambda v: v.tensor_tensor(
            out=sb[0:pdim, 0:1], in0=sb[0:pdim, 0:1], in1=sa[0:pdim, 0:1],
            op=OP.mult), reads=["rs_b", "rs_a"], writes=["rs_b"])
        p.op("vector", lambda v: v.tensor_scalar(
            out=sb[0:pdim, 0:1], in0=sb[0:pdim, 0:1], scalar1=-0.5,
            scalar2=1.5, op0=OP.mult, op1=OP.add),
            reads=["rs_b"], writes=["rs_b"])
        p.op("vector", lambda v: v.tensor_tensor(
            out=dst[0:pdim, 0:1], in0=dst[0:pdim, 0:1], in1=sb[0:pdim, 0:1],
            op=OP.mult), reads=["rs_d", "rs_b"], writes=["rs_d"])


def build():
    nc = bass.Bass("TRN2", target_bir_lowering=False)
    es = contextlib.ExitStack()
    p = Prog(nc, es)

    d_xfm = nc.dram_tensor("x_fm", [DM, TLOC + 3], F32, kind="ExternalInput").ap()
    d_xtok = nc.dram_tensor("x_tok", [TLOC, DM], F32, kind="ExternalInput").ap()
    d_wi = nc.dram_tensor("wi", [DM, 2 * DI], F32, kind="ExternalInput").ap()
    d_cw = nc.dram_tensor("cw", [128, NJU, CONV], F32, kind="ExternalInput").ap()
    d_cb = nc.dram_tensor("cb", [128, NJU], F32, kind="ExternalInput").ap()
    d_wx = nc.dram_tensor("wx", [DI, 96], F32, kind="ExternalInput").ap()
    d_wd = nc.dram_tensor("wd", [DTR, DI], F32, kind="ExternalInput").ap()
    d_bd = nc.dram_tensor("bd", [128, NJU], F32, kind="ExternalInput").ap()
    d_alogs = nc.dram_tensor("alogs", [128, NJB, NST], F32, kind="ExternalInput").ap()
    d_dp = nc.dram_tensor("dp", [128, NJU], F32, kind="ExternalInput").ap()
    d_wo = nc.dram_tensor("wo", [DI, DM], F32, kind="ExternalInput").ap()
    d_wgsh = nc.dram_tensor("wgsh", [IFF // 8, DM], F32, kind="ExternalInput").ap()
    d_wush = nc.dram_tensor("wush", [IFF // 8, DM], F32, kind="ExternalInput").ap()
    d_wdsh = nc.dram_tensor("wdsh", [DM // 8, IFF], F32, kind="ExternalInput").ap()
    d_dhv = nc.dram_tensor("dhv", [1, 1], I32, kind="ExternalInput").ap()
    d_out = nc.dram_tensor("out", [TLOC, DM], F32, kind="ExternalOutput").ap()

    ag_uc_in = nc.dram_tensor("ag_uc_in", [DI, TLOC], BF16)
    ag_uc_out = nc.dram_tensor("ag_uc_out", [2 * DI, TLOC], BF16)
    ag_dt_in = nc.dram_tensor("ag_dt_in", [DI, TLOC], BF16)
    ag_dt_out = nc.dram_tensor("ag_dt_out", [2 * DI, TLOC], BF16)
    ag_bc_in = nc.dram_tensor("ag_bc_in", [32, TLOC], BF16)
    ag_bc_out = nc.dram_tensor("ag_bc_out", [64, TLOC], BF16)
    ag_ys_in = nc.dram_tensor("ag_ys_in", [DI // 2, L], BF16)
    ag_ys_out = nc.dram_tensor("ag_ys_out", [DI, L], BF16)
    ucv = nc.dram_tensor("ucv", [DI // 2, L], BF16)
    dtv = nc.dram_tensor("dtv", [DI // 2, L], BF16)
    sz_priv = nc.dram_tensor("sz_priv", [DI, TLOC], BF16)
    x1_priv = nc.dram_tensor("x1_priv", [TLOC, DM], F32)
    wgq_in = nc.dram_tensor("wgq_in", [IFF // 8, DM], BF16)
    wgq_out = nc.dram_tensor("wgq_out", [IFF, DM], BF16, addr_space="Shared")
    wuq_in = nc.dram_tensor("wuq_in", [IFF // 8, DM], BF16)
    wuq_out = nc.dram_tensor("wuq_out", [IFF, DM], BF16, addr_space="Shared")
    wdq_in = nc.dram_tensor("wdq_in", [DM // 8, IFF], BF16)
    wdq_out = nc.dram_tensor("wdq_out", [DM, IFF], BF16, addr_space="Shared")

    taps = {}
    if DEBUG_TAPS:
        taps["t_uc"] = nc.dram_tensor("t_uc", [2 * DI, TLOC], BF16,
                                      kind="ExternalOutput").ap()
        taps["t_dt"] = nc.dram_tensor("t_dt", [2 * DI, TLOC], BF16,
                                      kind="ExternalOutput").ap()
        taps["t_bc"] = nc.dram_tensor("t_bc", [64, TLOC], BF16,
                                      kind="ExternalOutput").ap()
        taps["t_ys"] = nc.dram_tensor("t_ys", [DI, L], BF16,
                                      kind="ExternalOutput").ap()
        taps["t_x1"] = nc.dram_tensor("t_x1", [TLOC, DM], F32,
                                      kind="ExternalOutput").ap()

    A_sb = es.enter_context(nc.sbuf_tensor("A_sb", [128, NJB, NST], F32))
    dp_sb = es.enter_context(nc.sbuf_tensor("dp_sb", [128, NJU], F32))
    ident = es.enter_context(nc.sbuf_tensor("ident", [128, 128], BF16))
    dh_sb = es.enter_context(nc.sbuf_tensor("dh_sb", [1, 1], I32))
    mcp = es.enter_context(nc.sbuf_tensor("mcp", [128, 1], F32))
    mcn = es.enter_context(nc.sbuf_tensor("mcn", [128, 1], F32))
    rdh = es.enter_context(nc.gpsimd.register("rdh"))
    rt1 = es.enter_context(nc.gpsimd.register("rt1"))
    rt2 = es.enter_context(nc.gpsimd.register("rt2"))

    with contextlib.ExitStack() as es0:
        wstg = es0.enter_context(nc.sbuf_tensor("wstg", [128, IFF], F32))
        wqo = es0.enter_context(nc.sbuf_tensor("wqo", [128, IFF], BF16))
        scr0 = {k: es0.enter_context(nc.sbuf_tensor(f"q0_{k}", [128, IFF], F32))
                for k in ("vv", "av", "t1", "b2", "s16")}
        scr0["bm"] = es0.enter_context(nc.sbuf_tensor("q0_bm", [128, IFF // 16], F32))
        scr0["sc"] = es0.enter_context(nc.sbuf_tensor("q0_sc", [128, IFF // 16], F32))
        ones0 = es0.enter_context(nc.sbuf_tensor("ones0", [128, 128], F32))
        alog_sb = es0.enter_context(nc.sbuf_tensor("alog_sb", [128, NJB, NST], F32))

        p.dma("sync", dh_sb[:], d_dhv, writes=["dh_sb"])
        p.op("gpsimd", lambda g: g.reg_load(rdh, dh_sb[0:1, 0:1]),
             imm_reads=["dh_sb"])
        p.dma("sync", alog_sb[:], d_alogs, writes=["alog_sb"])
        p.dma("sync", dp_sb[:], d_dp, writes=["dp_sb"])
        p.op("scalar", lambda s: s.activation(
            out=A_sb.ap().rearrange("p j n -> p (j n)"),
            in_=alog_sb.ap().rearrange("p j n -> p (j n)"), func=AF.Exp),
            reads=["alog_sb"], writes=["A_sb"])
        p.op("vector", lambda v: v.tensor_scalar(
            out=A_sb.ap().rearrange("p j n -> p (j n)"),
            in0=A_sb.ap().rearrange("p j n -> p (j n)"), scalar1=-1.0,
            scalar2=None, op0=OP.mult), reads=["A_sb"], writes=["A_sb"])
        p.op("gpsimd", lambda g: g.memset(ones0[:], 1.0), writes=["ones0"])
        p.op("vector", lambda v: v.memset(mcp[:], MAGIC), writes=["mcol"])
        p.op("vector", lambda v: v.memset(mcn[:], -MAGIC), writes=["mcol"])
        p.op("gpsimd", lambda g: g.affine_select(
            out=ident[:], in_=ones0[:], pattern=[[1, 128]],
            compare_op=OP.is_equal, fill=0.0, base=0, channel_multiplier=-1),
            reads=["ones0"], writes=["ident"])

        def quant_rows(src_d, dst_d, nrows, fdim):
            r0 = 0
            while r0 < nrows:
                pr = min(128, nrows - r0)
                p.dma("sync", wstg[0:pr, 0:fdim], src_d[r0:r0 + pr, 0:fdim],
                      writes=["wstg"])
                emit_quant(p, wstg[0:pr, 0:fdim], wqo[0:pr, 0:fdim], pr, fdim,
                           scr0, "wstg", "wqo", mcols=(mcp, mcn))
                p.dma("sync", dst_d[r0:r0 + pr, 0:fdim], wqo[0:pr, 0:fdim],
                      reads=["wqo"], writes=[dst_d.tensor.name + "_d"])
                r0 += pr

        quant_rows(d_wgsh, wgq_in.ap(), IFF // 8, DM)
        quant_rows(d_wush, wuq_in.ap(), IFF // 8, DM)
        quant_rows(d_wdsh, wdq_in.ap(), DM // 8, IFF)

        p.op("gpsimd", lambda g: g.collective_compute(
            "AllGather", OP.bypass, replica_groups=ALLG,
            ins=[wgq_in.ap().opt()], outs=[wgq_out.ap().opt()]),
            reads=["wgq_in_d"], writes=["wgq_out_d"], sem_override="ccw0",
            inc_n=1)
        p.op("gpsimd", lambda g: g.collective_compute(
            "AllGather", OP.bypass, replica_groups=ALLG,
            ins=[wuq_in.ap().opt()], outs=[wuq_out.ap().opt()]),
            reads=["wuq_in_d"], writes=["wuq_out_d"], sem_override="ccw1",
            inc_n=1)
        p.op("gpsimd", lambda g: g.collective_compute(
            "AllGather", OP.bypass, replica_groups=ALLG,
            ins=[wdq_in.ap().opt()], outs=[wdq_out.ap().opt()]),
            reads=["wdq_in_d"], writes=["wdq_out_d"], sem_override="ccw2",
            inc_n=1)
        p.flush()

    if BUILD_PHASES == "0":
        p.drain_all()
        p.flush()
        es.close()
        return nc

    p.fence()
    with contextlib.ExitStack() as esA:
        wi_sb = esA.enter_context(nc.sbuf_tensor("wi_sb", [128, NKD, 2 * DI], BF16))
        wx_sb = esA.enter_context(nc.sbuf_tensor("wx_sb", [128, NJU, 96], BF16))
        wd_sb = esA.enter_context(nc.sbuf_tensor("wd_sb", [64, DI], BF16))
        diag = esA.enter_context(nc.sbuf_tensor("diag", [128, NJU, CONV, 128], BF16))
        cb_sb = esA.enter_context(nc.sbuf_tensor("cb_sb", [128, NJU], F32))
        bd_sb = esA.enter_context(nc.sbuf_tensor("bd_sb", [128, NJU], F32))
        cw_sb = esA.enter_context(nc.sbuf_tensor("cw_sb", [128, NJU, CONV], F32))
        wstgA2 = [esA.enter_context(nc.sbuf_tensor(f"wstgA{i}", [128, DI], F32))
                  for i in range(2)]
        onesA = esA.enter_context(nc.sbuf_tensor("onesA", [128, 128], F32))
        xst = esA.enter_context(nc.sbuf_tensor("xst", [128, NKD, TAU + 3], F32))
        xbf1 = esA.enter_context(nc.sbuf_tensor("xbf0", [128, NKD, TAU + 3], BF16))
        xbf2 = [xbf1, xbf1]
        u_sb = esA.enter_context(nc.sbuf_tensor("u_sb", [128, NJU, TAU + 3], BF16))
        uc_sb = esA.enter_context(nc.sbuf_tensor("uc_sb", [128, NJU, TAU], BF16))
        sz_sb = esA.enter_context(nc.sbuf_tensor("sz_sb", [128, NJU, TAU], BF16))
        dtlo_sb = esA.enter_context(nc.sbuf_tensor("dtlo_sb", [64, TAU], BF16))
        bc_sb = esA.enter_context(nc.sbuf_tensor("bc_sb", [32, TAU], BF16))
        dt_sb = esA.enter_context(nc.sbuf_tensor("dt_sb", [128, NJU, TAU], BF16))
        v_sb1 = esA.enter_context(nc.sbuf_tensor("v_sb0", [128, TAU], F32))
        v_sb2 = [v_sb1, v_sb1]
        va_sb1 = esA.enter_context(nc.sbuf_tensor("va_sb0", [128, TAU], F32))
        va_sb2 = [va_sb1, va_sb1]
        e_sb1 = esA.enter_context(nc.sbuf_tensor("e_sb0", [128, TAU], F32))
        e_sb2 = [e_sb1, e_sb1]
        l_sb1 = esA.enter_context(nc.sbuf_tensor("l_sb0", [128, TAU], F32))
        l_sb2 = [l_sb1, l_sb1]
        r_sb1 = esA.enter_context(nc.sbuf_tensor("r_sb0", [128, TAU], F32))
        r_sb2 = [r_sb1, r_sb1]
        psu = [esA.enter_context(nc.psum_tensor(f"psu{i}", [128, TAU], F32))
               for i in range(4)]
        pscv = [esA.enter_context(nc.psum_tensor(f"pscv{i}", [128, TAU], F32))
                for i in range(2)]
        ps_x = esA.enter_context(nc.psum_tensor("ps_x", [96, TAU], F32))

        for k in range(NKD):
            for hh in range(2):
                wp = (2 * k + hh) % 2
                wstgA = wstgA2[wp]
                p.dma("sync", wstgA[:, 0:DI],
                      d_wi[k * 128:(k + 1) * 128, hh * DI:(hh + 1) * DI],
                      writes=[f"wstgA{wp}"])
                p.op("vector", lambda v, k=k, hh=hh, wstgA=wstgA:
                     v.tensor_copy(
                         out=wi_sb[:, k, hh * DI:(hh + 1) * DI],
                         in_=wstgA[:, 0:DI]),
                     reads=[f"wstgA{wp}"], writes=["wi_sb"])
        p.dma("sync", wstgA2[0][:, 0:NJU * 96].rearrange("p (j n) -> p j n", n=96),
              d_wx.rearrange("(j p) n -> p j n", p=128), writes=["wstgA0"])
        p.op("vector", lambda v: v.tensor_copy(
            out=wx_sb.ap().rearrange("p j n -> p (j n)"),
            in_=wstgA2[0][:, 0:NJU * 96]), reads=["wstgA0"], writes=["wx_sb"])
        p.dma("sync", wstgA2[1][0:64, 0:DI], d_wd, writes=["wstgA1"])
        p.op("vector", lambda v: v.tensor_copy(
            out=wd_sb[:], in_=wstgA2[1][0:64, 0:DI]),
            reads=["wstgA1"], writes=["wd_sb"])
        p.dma("sync", cw_sb[:], d_cw, writes=["cw_sb"])
        p.dma("sync", cb_sb[:], d_cb, writes=["cb_sb"])
        p.dma("sync", bd_sb[:], d_bd, writes=["bd_sb"])
        p.op("gpsimd", lambda g: g.memset(onesA[:], 1.0), writes=["onesA"])
        for j in range(NJU):
            for k in range(CONV):
                p.op("gpsimd", lambda g, j=j, k=k: g.tensor_scalar(
                    out=diag[:, j, k, :], in0=onesA[:],
                    scalar1=cw_sb[:, j, k:k + 1], scalar2=None, op0=OP.mult),
                    reads=["onesA"], imm_reads=["cw_sb"], writes=["diag"])
        for j in range(NJU):
            for k in range(CONV):
                p.op("gpsimd", lambda g, j=j, k=k: g.affine_select(
                    out=diag[:, j, k, :], in_=diag[:, j, k, :],
                    pattern=[[1, 128]], compare_op=OP.is_equal, fill=0.0,
                    base=0, channel_multiplier=-1),
                    reads=["diag"], writes=["diag"])

        if BUILD_PHASES == "A1":
            p.drain_all()
        NT = 0 if BUILD_PHASES == "A1" else TLOC // TAU
        for i in range(min(NT, 1) if BUILD_PHASES == "A2" else NT):
            c0 = i * TAU
            ip = i % 2
            xbf = xbf2[ip]
            B_xst, B_xbf = "xst", "xbf0"
            p.dma("sync", xst[:], d_xfm.rearrange("(k p) t -> p k t", p=128)
                  [:, :, c0:c0 + TAU + 3], writes=[B_xst])
            p.op("vector", lambda v, xbf=xbf, xst=xst: v.tensor_copy(
                out=xbf[:], in_=xst[:]), reads=[B_xst], writes=[B_xbf])
            if i > 0:
                p.op("vector", lambda v: v.tensor_copy(
                    out=u_sb[:, :, 0:3], in_=u_sb[:, :, TAU:TAU + 3]),
                    reads=["u_sb"], writes=["u_sb"])
            for j in range(NJI):
                is_u = j < NJU
                bank = psu[j % 4]
                bname = f"psu{j % 4}"

                def mm_ip(t, j=j, bank=bank, xbf=xbf):
                    r = None
                    for k in range(NKD):
                        r = t.matmul(bank[:, 0:TAU],
                                     wi_sb[:, k, j * 128:(j + 1) * 128],
                                     xbf[:, k, 3:3 + TAU],
                                     start=(k == 0), stop=(k == NKD - 1))
                    return r
                p.op("tensor", mm_ip, reads=["wi_sb", B_xbf], writes=[bname])
                if is_u:
                    p.op("vector", lambda v, j=j, bank=bank: v.tensor_copy(
                        out=u_sb[:, j, 3:3 + TAU], in_=bank[:, 0:TAU]),
                        reads=[bname], writes=["u_sb"])
                else:
                    p.op("scalar", lambda s, j=j, bank=bank: s.activation(
                        out=sz_sb[:, j - NJU, :], in_=bank[:, 0:TAU],
                        func=AF.Silu), reads=[bname], writes=["sz_sb"])
            if i == 0:
                for j in range(NJU):
                    bank = psu[j % 4]
                    bname = f"psu{j % 4}"

                    def mm_h(t, j=j, bank=bank, xbf=xbf):
                        r = None
                        for k in range(NKD):
                            r = t.matmul(bank[:, 0:3],
                                         wi_sb[:, k, j * 128:(j + 1) * 128],
                                         xbf[:, k, 0:3],
                                         start=(k == 0), stop=(k == NKD - 1))
                        return r
                    p.op("tensor", mm_h, reads=["wi_sb", B_xbf],
                         writes=[bname])
                    p.op("vector", lambda v, j=j, bank=bank: v.tensor_copy(
                        out=u_sb[:, j, 0:3], in_=bank[:, 0:3]),
                        reads=[bname], writes=["u_sb"])
            for jc in range(0, NJU, 4):
                p.dma("scalar", sz_priv.ap()
                      .rearrange("(j p) t -> p j t", p=128)
                      [:, jc:jc + 4, c0:c0 + TAU], sz_sb[:, jc:jc + 4, :],
                      reads=["sz_sb"], writes=["sz_priv_d"])
            for j in range(NJU):
                bank = pscv[j % 2]
                bname = f"pscv{j % 2}"

                def mm_cv(t, j=j, bank=bank):
                    r = None
                    for k in range(CONV):
                        r = t.matmul(bank[:, 0:TAU], diag[:, j, k, :],
                                     u_sb[:, j, k:k + TAU],
                                     start=(k == 0), stop=(k == CONV - 1))
                    return r
                p.op("tensor", mm_cv, reads=["diag", "u_sb"], writes=[bname])
                p.op("scalar", lambda s, j=j, bank=bank: s.activation(
                    out=uc_sb[:, j, :], in_=bank[:, 0:TAU], func=AF.Silu,
                    bias=cb_sb[:, j:j + 1]), reads=[bname],
                    imm_reads=["cb_sb"], writes=["uc_sb"])
            for jc in range(0, NJU, 4):
                p.dma("scalar", ag_uc_in.ap()
                      .rearrange("(j p) t -> p j t", p=128)
                      [:, jc:jc + 4, c0:c0 + TAU], uc_sb[:, jc:jc + 4, :],
                      reads=["uc_sb"], writes=["ag_uc_in_d"])

            def mm_xp(t):
                r = None
                for k in range(NJU):
                    r = t.matmul(ps_x[:, 0:TAU], wx_sb[:, k, :],
                                 uc_sb[:, k, :],
                                 start=(k == 0), stop=(k == NJU - 1))
                return r
            p.op("tensor", mm_xp, reads=["wx_sb", "uc_sb"], writes=["ps_x"])
            p.op("vector", lambda v: v.tensor_copy(
                out=dtlo_sb[:], in_=ps_x[0:64, 0:TAU]),
                reads=["ps_x"], writes=["dtlo_sb"])
            p.op("vector", lambda v: v.tensor_copy(
                out=bc_sb[:], in_=ps_x[64:96, 0:TAU]),
                reads=["ps_x"], writes=["bc_sb"])
            p.dma("scalar", ag_bc_in.ap()[0:32, c0:c0 + TAU], bc_sb[:],
                  reads=["bc_sb"], writes=["ag_bc_in_d"])
            def sp_front(j):
                jp = j % 2
                bank = psu[jp]
                bname = f"psu{jp}"
                v_sb, va_sb = v_sb2[jp], va_sb2[jp]
                p.op("tensor", lambda t, j=j, bank=bank: t.matmul(
                    bank[:, 0:TAU], wd_sb[:, j * 128:(j + 1) * 128],
                    dtlo_sb[:], start=True, stop=True),
                    reads=["wd_sb", "dtlo_sb"], writes=[bname])
                p.op("vector", lambda v, j=j, bank=bank, v_sb=v_sb:
                     v.tensor_scalar(
                         out=v_sb[:], in0=bank[:, 0:TAU],
                         scalar1=bd_sb[:, j:j + 1], scalar2=None, op0=OP.add),
                     reads=[bname], imm_reads=["bd_sb"], writes=["v_sb0"])
                p.op("vector", lambda v, v_sb=v_sb, va_sb=va_sb:
                     v.tensor_scalar(
                         out=va_sb.ap().bitcast(I32)[:, :],
                         in0=v_sb.ap().bitcast(I32)[:, :],
                         scalar1=0x7FFFFFFF, scalar2=None,
                         op0=OP.bitwise_and),
                     reads=["v_sb0"], writes=["va_sb0"])
                p.op("scalar", lambda s, va_sb=va_sb, e_sb=e_sb2[jp]:
                     s.activation(
                         out=e_sb[:], in_=va_sb[:], func=AF.Exp, scale=-1.0),
                     reads=["va_sb0"], writes=["e_sb0"])

            def sp_back(j):
                jp = j % 2
                v_sb, e_sb, l_sb, r_sb = (v_sb2[jp], e_sb2[jp], l_sb2[jp],
                                          r_sb2[jp])
                p.op("scalar", lambda s, e_sb=e_sb, l_sb=l_sb: s.activation(
                    out=l_sb[:], in_=e_sb[:], func=AF.Ln, bias=1.0),
                    reads=["e_sb0"], writes=["l_sb0"])
                p.op("vector", lambda v, v_sb=v_sb, r_sb=r_sb: v.tensor_scalar(
                    out=r_sb[:], in0=v_sb[:], scalar1=0.0, scalar2=None,
                    op0=OP.max), reads=["v_sb0"], writes=["r_sb0"])
                p.op("vector", lambda v, j=j, l_sb=l_sb, r_sb=r_sb:
                     v.tensor_tensor(
                         out=dt_sb[:, j, :], in0=l_sb[:], in1=r_sb[:],
                         op=OP.add),
                     reads=["l_sb0", "r_sb0"], writes=["dt_sb"])

            for j in range(NJU + 1):
                if j > 0:
                    sp_back(j - 1)
                if j < NJU:
                    sp_front(j)
            for jc in range(0, NJU, 4):
                p.dma("scalar", ag_dt_in.ap()
                      .rearrange("(j p) t -> p j t", p=128)
                      [:, jc:jc + 4, c0:c0 + TAU], dt_sb[:, jc:jc + 4, :],
                      reads=["dt_sb"], writes=["ag_dt_in_d"])

        if BUILD_PHASES in ("A1", "A2", "A3"):
            p.drain_all()
            p.flush()
            _SKIP_REST[0] = True
        if not _SKIP_REST[0]:
            p.op("gpsimd", lambda g: g.collective_compute(
                    "AllGather", OP.bypass, replica_groups=PAIRS,
                    ins=[ag_uc_in.ap().opt()], outs=[ag_uc_out.ap().opt()]),
                reads=["ag_uc_in_d"], writes=["ag_uc_out_d"], sem_override="cca0",
                inc_n=1)
            p.op("gpsimd", lambda g: g.collective_compute(
                "AllGather", OP.bypass, replica_groups=PAIRS,
                ins=[ag_dt_in.ap().opt()], outs=[ag_dt_out.ap().opt()]),
                reads=["ag_dt_in_d"], writes=["ag_dt_out_d"], sem_override="cca1",
                inc_n=1)
            p.op("gpsimd", lambda g: g.collective_compute(
                "AllGather", OP.bypass, replica_groups=PAIRS,
                ins=[ag_bc_in.ap().opt()], outs=[ag_bc_out.ap().opt()]),
                reads=["ag_bc_in_d"], writes=["ag_bc_out_d"], sem_override="cca2",
                inc_n=1)
            _rg = [0]
            for (gout, gname, vout, vname) in (
                    (ag_uc_out, "ag_uc_out_d", ucv, "ucv_d"),
                    (ag_dt_out, "ag_dt_out_d", dtv, "dtv_d")):
                for r in range(2):
                    def resl(g, gout=gout, vout=vout, r=r):
                        g.reg_mul(rt1, rdh, 1024 * TLOC)
                        g.reg_add(rt1, rt1, r * DI * TLOC)
                        return g.dma_start(
                            out=vout.ap()[:, r * TLOC:(r + 1) * TLOC],
                            in_=bass.AP(gout, rt1, [[TLOC, 1024], [1, TLOC]]))
                    p.op("gpsimd", resl, reads=[gname], writes=[vname],
                         dma=True, sem_override=f"rg{_rg[0]}")
                    _rg[0] += 1
            if DEBUG_TAPS:
                p.dma("sync", taps["t_uc"], ag_uc_out.ap(),
                      reads=["ag_uc_out_d"], writes=["tap_uc"])
                p.dma("sync", taps["t_dt"], ag_dt_out.ap(),
                      reads=["ag_dt_out_d"], writes=["tap_dt"])
                p.dma("sync", taps["t_bc"], ag_bc_out.ap(),
                      reads=["ag_bc_out_d"], writes=["tap_bc"])
            p.flush()

    if BUILD_PHASES in ("A", "A1", "A2", "A3"):
        p.drain_all()
        p.flush()
        es.close()
        return nc

    wgt = es.enter_context(nc.sbuf_tensor("wgt", [128, NKD, IFF], BF16))

    p.fence()
    with contextlib.ExitStack() as esB:
        bbc = esB.enter_context(nc.sbuf_tensor("bbc", [128, NST, TC], BF16))
        cbc2 = esB.enter_context(nc.sbuf_tensor("cbc2", [128, NST, TC], BF16))
        dt_t2 = [esB.enter_context(nc.sbuf_tensor(f"dt_t{i}", [128, TC], BF16))
                 for i in range(2)]
        uc_t2 = [esB.enter_context(nc.sbuf_tensor(f"uc_t{i}", [128, TC], BF16))
                 for i in range(2)]
        w_t2 = [esB.enter_context(nc.sbuf_tensor(f"w_t{i}", [128, TC], BF16))
                for i in range(2)]
        dA_t2 = [esB.enter_context(nc.sbuf_tensor(f"dA_t{i}", [128, TC], BF16))
                 for i in range(4)]
        X_t2 = [esB.enter_context(nc.sbuf_tensor(f"X_t{i}", [128, TC], BF16))
                for i in range(2)]
        P16 = esB.enter_context(nc.sbuf_tensor("P16", [128, NST, TC], BF16))
        ysb = esB.enter_context(nc.sbuf_tensor("ysb", [128, TC], BF16))
        stc = esB.enter_context(nc.sbuf_tensor("stc", [128, NJB * NST], F32))

        p.op("vector", lambda v: v.memset(stc[:], 0.0), writes=["stc"])
        p.op("sync", lambda e: e.dma_start(
            out=wgt.ap(), in_=wgq_out.ap(), transpose=True),
            reads=["wgq_out_d"], writes=["wgt"], dma=True)
        for tc_i in range(L // TC):
            rank = tc_i // (TLOC // TC)
            coff = (tc_i % (TLOC // TC)) * TC
            gcol = rank * TLOC + coff
            col = tc_i * TC
            for n in range(NST):
                p.dma("sync", bbc[:, n, :],
                      ag_bc_out.ap()[rank * 32 + n: rank * 32 + n + 1,
                                     coff:coff + TC].broadcast_to([128, TC]),
                      reads=["ag_bc_out_d"], writes=["bbc"])
                p.dma("sync", cbc2[:, n, :],
                      ag_bc_out.ap()[rank * 32 + NST + n:
                                     rank * 32 + NST + n + 1,
                                     coff:coff + TC].broadcast_to([128, TC]),
                      reads=["ag_bc_out_d"], writes=["cbc2"])
            for j in range(NJB):
                jp = j % 2
                dt_t, uc_t, w_t = dt_t2[jp], uc_t2[jp], w_t2[jp]
                B_dt, B_uc, B_w = f"dt_t{jp}", f"uc_t{jp}", f"w_t{jp}"
                p.dma("sync", dt_t[:], dtv.ap()[j * 128:(j + 1) * 128,
                                                col:col + TC],
                      reads=["dtv_d"], writes=[B_dt])
                p.dma("sync", uc_t[:], ucv.ap()[j * 128:(j + 1) * 128,
                                                col:col + TC],
                      reads=["ucv_d"], writes=[B_uc])
                p.op("vector", lambda v, dt_t=dt_t, uc_t=uc_t, w_t=w_t:
                     v.tensor_tensor(
                         out=w_t[:], in0=dt_t[:], in1=uc_t[:], op=OP.mult),
                     reads=[B_dt, B_uc], writes=[B_w])
                for n in range(NST):
                    npp = n % 2
                    dA_t = dA_t2[n % 4]
                    dA_name = f"dA_t{n % 4}"
                    X_t = X_t2[npp]
                    B_X = f"X_t{npp}"
                    p.op("scalar", lambda s, j=j, n=n, dA_t=dA_t, dt_t=dt_t:
                         s.activation(
                             out=dA_t[:], in_=dt_t[:], func=AF.Exp,
                             scale=A_sb[:, j, n:n + 1]),
                         reads=[B_dt], imm_reads=["A_sb"], writes=[dA_name])
                    p.op("vector", lambda v, n=n, X_t=X_t, w_t=w_t:
                         v.tensor_tensor(
                             out=X_t[:], in0=w_t[:], in1=bbc[:, n, :],
                             op=OP.mult),
                         reads=[B_w, "bbc"], writes=[B_X])
                    cidx = j * NST + n
                    p.op("vector", lambda v, cidx=cidx, dA_t=dA_t, n=n,
                         X_t=X_t: v.tensor_tensor_scan(
                             out=P16[:, n, :], data0=dA_t[:], data1=X_t[:],
                             initial=stc[:, cidx:cidx + 1],
                             op0=OP.mult, op1=OP.add),
                         reads=[dA_name, B_X], imm_reads=["stc"],
                         writes=["P16"])
                    p.op("vector", lambda v, cidx=cidx, n=n: v.tensor_copy(
                        out=stc[:, cidx:cidx + 1],
                        in_=P16[:, n, TC - 1:TC]),
                        reads=["P16"], writes=["stc"])
                p.op("vector", lambda v: v.tensor_tensor(
                    out=P16.ap(), in0=P16.ap(), in1=cbc2.ap(), op=OP.mult),
                    reads=["P16", "cbc2"], writes=["P16"])
                for lev, step in ((1, 2), (2, 4), (3, 8)):
                    ng = NST // (2 ** lev)
                    v0 = P16.ap().rearrange("p (a b) t -> p a b t", b=step)
                    p.op("vector", lambda v, v0=v0, ng=ng, step=step:
                         v.tensor_tensor(
                             out=v0[:, :, 0, :], in0=v0[:, :, 0, :],
                             in1=v0[:, :, step // 2, :], op=OP.add),
                         reads=["P16"], writes=["P16"])
                p.op("vector", lambda v: v.tensor_tensor(
                    out=ysb[:], in0=P16[:, 0, :], in1=P16[:, 8, :],
                    op=OP.add), reads=["P16"], writes=["ysb"])
                p.dma("scalar", ag_ys_in.ap()[j * 128:(j + 1) * 128,
                                              col:col + TC], ysb[:],
                      reads=["ysb"], writes=["ag_ys_in_d"])
        p.op("gpsimd", lambda g: g.collective_compute(
            "AllGather", OP.bypass, replica_groups=PAIRS,
            ins=[ag_ys_in.ap().opt()], outs=[ag_ys_out.ap().opt()]),
            reads=["ag_ys_in_d"], writes=["ag_ys_out_d"], sem_override="ccy0",
            inc_n=1)
        if DEBUG_TAPS:
            p.dma("sync", taps["t_ys"], ag_ys_out.ap(),
                  reads=["ag_ys_out_d"], writes=["tap_ys"])
        p.flush()

    if BUILD_PHASES == "AB":
        p.drain_all()
        p.flush()
        es.close()
        return nc

    wut = es.enter_context(nc.sbuf_tensor("wut", [128, NKD, IFF], BF16))

    p.fence()
    with contextlib.ExitStack() as esC1:
        wo_sb = esC1.enter_context(nc.sbuf_tensor("wo_sb", [128, NJU, DM], BF16))
        wstgC = esC1.enter_context(nc.sbuf_tensor("wstgC", [128, DM // 8], F32))
        ys_sb1a = esC1.enter_context(nc.sbuf_tensor("ys_sb0", [128, NJU, TAU], BF16))
        ys_sb2 = [ys_sb1a, ys_sb1a]
        ucC1a = esC1.enter_context(nc.sbuf_tensor("ucC0", [128, NJU, TAU], BF16))
        ucC2 = [ucC1a, ucC1a]
        szC1a = esC1.enter_context(nc.sbuf_tensor("szC0", [128, NJU, TAU], BF16))
        szC2 = [szC1a, szC1a]
        y2 = esC1.enter_context(nc.sbuf_tensor("y2", [128, TAU], F32))
        y3 = esC1.enter_context(nc.sbuf_tensor("y3", [128, NJU, TAU], BF16))
        xtC1a = esC1.enter_context(nc.sbuf_tensor("xtC0", [128, TAU // 128, DM], F32))
        xtC2 = [xtC1a, xtC1a]
        s_sb = esC1.enter_context(nc.sbuf_tensor("s_sb", [128, DM], F32))
        mu = esC1.enter_context(nc.sbuf_tensor("mu", [128, 1], F32))
        s2c = esC1.enter_context(nc.sbuf_tensor("s2c", [128, 1], F32))
        var = esC1.enter_context(nc.sbuf_tensor("var", [128, 1], F32))
        rstd = esC1.enter_context(nc.sbuf_tensor("rstd", [128, 1], F32))
        rs_a = esC1.enter_context(nc.sbuf_tensor("rs_a", [128, 1], F32))
        rs_b = esC1.enter_context(nc.sbuf_tensor("rs_b", [128, 1], F32))
        pso = [esC1.enter_context(nc.psum_tensor(f"pso{i}", [128, 512], F32))
               for i in range(2)]

        p.op("sync", lambda e: e.dma_start(
            out=wut.ap(), in_=wuq_out.ap(), transpose=True),
            reads=["wuq_out_d"], writes=["wut"], dma=True)
        for k in range(NJU):
            for qq in range(8):
                p.dma("sync", wstgC[:, 0:DM // 8],
                      d_wo[k * 128:(k + 1) * 128,
                           qq * (DM // 8):(qq + 1) * (DM // 8)],
                      writes=["wstgC"])
                p.op("vector", lambda v, k=k, qq=qq: v.tensor_copy(
                    out=wo_sb[:, k, qq * (DM // 8):(qq + 1) * (DM // 8)],
                    in_=wstgC[:, 0:DM // 8]),
                    reads=["wstgC"], writes=["wo_sb"])

        NT = TLOC // TAU
        for i in range(NT):
            c0 = i * TAU
            ip = i % 2
            ys_sb, ucC, szC, xtC = ys_sb2[ip], ucC2[ip], szC2[ip], xtC2[ip]
            B_ys, B_ucC, B_szC, B_xtC = ("ys_sb0", "ucC0",
                                         "szC0", "xtC0")

            def ys_load(g, i=i, ys_sb=ys_sb):
                g.reg_mul(rt2, rdh, TLOC)
                g.reg_add(rt2, rt2, i * TAU)
                return g.dma_start(
                    out=ys_sb[:],
                    in_=bass.AP(ag_ys_out, rt2,
                                [[L, 128], [128 * L, NJU], [1, TAU]]))
            p.op("gpsimd", ys_load, reads=["ag_ys_out_d"], writes=[B_ys],
                 dma=True, sem_override=f"ry{i}")
            p.dma("sync", ucC[:], ag_uc_in.ap()
                  .rearrange("(j p) t -> p j t", p=128)[:, :, c0:c0 + TAU],
                  reads=["ag_uc_in_d"], writes=[B_ucC])
            p.dma("sync", szC[:], sz_priv.ap()
                  .rearrange("(j p) t -> p j t", p=128)[:, :, c0:c0 + TAU],
                  reads=["sz_priv_d"], writes=[B_szC])
            p.dma("sync", xtC[:], d_xtok[c0:c0 + TAU, :]
                  .rearrange("(q p) m -> p q m", p=128), writes=[B_xtC])
            for j in range(NJU):
                p.op("vector", lambda v, j=j, ucC=ucC, ys_sb=ys_sb:
                     v.scalar_tensor_tensor(
                         out=y2[:], in0=ucC[:, j, :], scalar=dp_sb[:, j:j + 1],
                         in1=ys_sb[:, j, :], op0=OP.mult, op1=OP.add),
                     reads=[B_ucC, B_ys], imm_reads=["dp_sb"],
                     writes=["y2"])
                p.op("vector", lambda v, j=j, szC=szC: v.tensor_tensor(
                    out=y3[:, j, :], in0=y2[:], in1=szC[:, j, :], op=OP.mult),
                    reads=["y2", B_szC], writes=["y3"])
            for q in range(TAU // 128):
                for half in range(2):
                    bank = pso[half]
                    bname = f"pso{half}"

                    def mm_o(t, q=q, half=half, bank=bank):
                        r = None
                        for k in range(NJU):
                            r = t.matmul(
                                bank[:, 0:512],
                                y3[:, k, q * 128:(q + 1) * 128],
                                wo_sb[:, k, half * 512:(half + 1) * 512],
                                start=(k == 0), stop=(k == NJU - 1))
                        return r
                    p.op("tensor", mm_o, reads=["y3", "wo_sb"],
                         writes=[bname])
                    p.op("vector", lambda v, q=q, half=half, bank=bank,
                         xtC=xtC: v.tensor_tensor(
                             out=s_sb[:, half * 512:(half + 1) * 512],
                             in0=bank[:, 0:512],
                             in1=xtC[:, q, half * 512:(half + 1) * 512],
                             op=OP.add),
                         reads=[bname, B_xtC], writes=["s_sb"])
                p.op("vector", lambda v: v.tensor_reduce(
                    out=mu[:], in_=s_sb[:], axis=AX.X, op=OP.add),
                    reads=["s_sb"], writes=["mu"])
                p.op("scalar", lambda s: s.activation(
                    out=y2[:], in_=s_sb[:, 0:512], func=AF.Square,
                    accum_out=s2c[:]), reads=["s_sb"],
                    writes=["y2", "s2c"])
                p.op("scalar", lambda s: s.activation(
                    out=y2[:], in_=s_sb[:, 512:1024], func=AF.Square,
                    accum_out=rs_a[:]), reads=["s_sb"],
                    writes=["y2", "rs_a"])
                p.op("vector", lambda v: v.tensor_tensor(
                    out=s2c[:], in0=s2c[:], in1=rs_a[:], op=OP.add),
                    reads=["s2c", "rs_a"], writes=["s2c"])
                p.op("vector", lambda v: v.tensor_scalar(
                    out=mu[:], in0=mu[:], scalar1=1.0 / DM, scalar2=None,
                    op0=OP.mult), reads=["mu"], writes=["mu"])
                p.op("vector", lambda v: v.tensor_scalar(
                    out=var[:], in0=s2c[:], scalar1=1.0 / DM, scalar2=None,
                    op0=OP.mult), reads=["s2c"], writes=["var"])
                p.op("vector", lambda v: v.tensor_tensor(
                    out=rs_b[:], in0=mu[:], in1=mu[:], op=OP.mult),
                    reads=["mu"], writes=["rs_b"])
                p.op("vector", lambda v: v.tensor_tensor(
                    out=var[:], in0=var[:], in1=rs_b[:], op=OP.subtract),
                    reads=["var", "rs_b"], writes=["var"])
                emit_rsqrt(p, rstd, "rs_d", var[:], "var", 1e-5, rs_a, rs_b,
                           128)
                p.op("vector", lambda v: v.tensor_scalar(
                    out=s_sb[:], in0=s_sb[:], scalar1=mu[:, 0:1],
                    scalar2=rstd[:, 0:1], op0=OP.subtract, op1=OP.mult),
                    reads=["s_sb"], imm_reads=["mu", "rs_d"],
                    writes=["s_sb"])
                p.dma("scalar", x1_priv.ap()[c0 + q * 128:c0 + (q + 1) * 128, :],
                      s_sb[:], reads=["s_sb"], writes=["x1_priv_d"])
        if DEBUG_TAPS:
            p.dma("sync", taps["t_x1"], x1_priv.ap(),
                  reads=["x1_priv_d"], writes=["tap_x1"])
        p.flush()

    if BUILD_PHASES == "ABC1":
        p.drain_all()
        p.flush()
        es.close()
        return nc

    p.fence()
    IC = [512, 512, 512, 512, 512, 256]
    with contextlib.ExitStack() as esC2:
        wdt = esC2.enter_context(nc.sbuf_tensor("wdt", [128, IFF // 128, DM], BF16))
        x1t2 = [esC2.enter_context(nc.sbuf_tensor(f"x1t{i}", [128, DM], F32))
                for i in range(2)]
        xq1 = esC2.enter_context(nc.sbuf_tensor("xq0", [128, DM], BF16))
        xq2 = [xq1, xq1]
        xqf1 = esC2.enter_context(nc.sbuf_tensor("xqf0", [128, NKD, 128], BF16))
        xqf2 = [xqf1, xqf1]
        sg1 = esC2.enter_context(nc.sbuf_tensor("sg2_0", [128, 1024], BF16))
        sg2 = [sg1, sg1]
        esC2_hfp = [esC2.enter_context(
            nc.sbuf_tensor(f"hfp{i}", [128, 1024], F32)) for i in range(2)]
        hq1 = esC2.enter_context(nc.sbuf_tensor("hq0", [128, IFF], BF16))
        hq2 = [hq1, hq1]
        hqf1 = esC2.enter_context(nc.sbuf_tensor("hqf", [128, IFF // 128, 128], BF16))
        hqf2 = [hqf1, hqf1]
        o_sb1 = esC2.enter_context(nc.sbuf_tensor("o_sb0", [128, DM], F32))
        o_sb2 = [o_sb1, o_sb1]
        qh_bm = esC2.enter_context(nc.sbuf_tensor("qhs_bm", [128, 64], F32))
        qh_sc = esC2.enter_context(nc.sbuf_tensor("qhs_sc", [128, 64], F32))
        qh_s16 = esC2.enter_context(nc.sbuf_tensor("qhs_s16", [128, 1024], F32))
        scrh2 = []
        for si in range(2):
            d = {k: esC2.enter_context(
                nc.sbuf_tensor(f"qh{si}_{k}", [128, 1024], F32))
                for k in ("vv", "av", "t1", "b2")}
            d["bm"] = qh_bm
            d["sc"] = qh_sc
            d["s16"] = qh_s16
            scrh2.append(d)
        muC = esC2.enter_context(nc.sbuf_tensor("muC", [128, 1], F32))
        s2C = esC2.enter_context(nc.sbuf_tensor("s2C", [128, 1], F32))
        varC = esC2.enter_context(nc.sbuf_tensor("varC", [128, 1], F32))
        rstdC = esC2.enter_context(nc.sbuf_tensor("rstdC", [128, 1], F32))
        rs_a2 = esC2.enter_context(nc.sbuf_tensor("rs_a2", [128, 1], F32))
        rs_b2 = esC2.enter_context(nc.sbuf_tensor("rs_b2", [128, 1], F32))
        psg2 = [esC2.enter_context(nc.psum_tensor(f"psg{i}", [128, 512], F32))
                for i in range(2)]
        psu2 = [esC2.enter_context(nc.psum_tensor(f"psup{i}", [128, 512], F32))
                for i in range(2)]
        psd = [esC2.enter_context(nc.psum_tensor(f"psd{i}", [128, 512], F32))
               for i in range(2)]
        pst = [esC2.enter_context(nc.psum_tensor(f"pst{i}", [128, 128], BF16))
               for i in range(2)]

        p.op("sync", lambda e: e.dma_start(
            out=wdt.ap(), in_=wdq_out.ap(), transpose=True),
            reads=["wdq_out_d"], writes=["wdt"], dma=True)

        for tt in range(TLOC // 128):
            r0 = tt * 128
            pp = tt % 2
            x1t, xq, xqf = x1t2[pp], xq2[pp], xqf2[pp]
            sg_sb, hq, hqf, o_sb = sg2[pp], hq2[pp], hqf2[pp], o_sb2[pp]
            B_x1, B_xq, B_xqf = f"x1t{pp}", "xq0", "xqf0"
            B_sg, B_hq, B_hqf, B_o = ("sg2_0", "hq0", "hqf", "o_sb0")

            p.dma("sync", x1t[:], x1_priv.ap()[r0:r0 + 128, :],
                  reads=["x1_priv_d"], writes=[B_x1])
            emit_quant(p, x1t[:, 0:DM], xq[:, 0:DM], 128, DM,
                       scrh2[pp], B_x1, B_xq, pfx=f"qh{pp}",
                       mcols=(mcp, mcn), bs_pfx="qhs")
            for g in range(NKD):
                bank = pst[g % 2]
                bname = f"pst{g % 2}"
                p.op("tensor", lambda t, g=g, bank=bank, xq=xq: t.transpose(
                    bank[:], xq[:, g * 128:(g + 1) * 128], ident[:]),
                    reads=[B_xq, "ident"], writes=[bname])
                p.op("scalar", lambda s, g=g, bank=bank, xqf=xqf: s.copy(
                    out=xqf[:, g, :], in_=bank[:]),
                    reads=[bname], writes=[B_xqf])
            PAIRS_IC = [(0, (512, 512)), (1024, (512, 512)),
                        (2048, (512, 256))]
            hf_p2 = [esC2_hfp[0], esC2_hfp[1], esC2_hfp[0]]
            pend = []
            for pi in range(len(PAIRS_IC) + 1):
                if pi < len(PAIRS_IC):
                    poff, subws = PAIRS_IC[pi]
                    hf = hf_p2[pi]
                    B_hf = f"hfp{pi % 2}"
                    so = 0
                    for sb_i, sw in enumerate(subws):
                        bank_g, bank_u = psg2[sb_i], psu2[sb_i]
                        bn_g, bn_u = f"psg{sb_i}", f"psup{sb_i}"
                        io = poff + so

                        def mm_g(t, io=io, sw=sw, xqf=xqf, bank=bank_g):
                            r = None
                            for k in range(NKD):
                                r = t.matmul(bank[:, 0:sw], xqf[:, k, :],
                                             wgt[:, k, io:io + sw],
                                             start=(k == 0),
                                             stop=(k == NKD - 1))
                            return r
                        p.op("tensor", mm_g, reads=[B_xqf, "wgt"],
                             writes=[bn_g])

                        def mm_u(t, io=io, sw=sw, xqf=xqf, bank=bank_u):
                            r = None
                            for k in range(NKD):
                                r = t.matmul(bank[:, 0:sw], xqf[:, k, :],
                                             wut[:, k, io:io + sw],
                                             start=(k == 0),
                                             stop=(k == NKD - 1))
                            return r
                        p.op("tensor", mm_u, reads=[B_xqf, "wut"],
                             writes=[bn_u])
                        p.op("scalar", lambda s, so=so, sw=sw, bank=bank_g,
                             sg_sb=sg_sb: s.activation(
                                 out=sg_sb[:, so:so + sw], in_=bank[:, 0:sw],
                                 func=AF.Silu),
                             reads=[bn_g], writes=[B_sg])
                        p.op("vector", lambda v, so=so, sw=sw, sg_sb=sg_sb,
                             bank=bank_u, hf=hf: v.tensor_tensor(
                                 out=hf[:, so:so + sw],
                                 in0=sg_sb[:, so:so + sw],
                                 in1=bank[:, 0:sw], op=OP.mult),
                             reads=[B_sg, bn_u], writes=[B_hf])
                        so += sw
                    pend.append((pi, poff, sum(subws), hf, B_hf))
                if pi > 0:
                    qpi, qpoff, qpw, qhf, qB_hf = pend.pop(0)
                    emit_quant(p, qhf[:, 0:qpw], hq[:, qpoff:qpoff + qpw],
                               128, qpw, scrh2[qpi % 2], qB_hf, B_hq,
                               pfx=f"qh{qpi % 2}", mcols=(mcp, mcn),
                               bs_pfx="qhs")
            for g in range(IFF // 128):
                bank = pst[g % 2]
                bname = f"pst{g % 2}"
                p.op("tensor", lambda t, g=g, bank=bank, hq=hq: t.transpose(
                    bank[:], hq[:, g * 128:(g + 1) * 128], ident[:]),
                    reads=[B_hq, "ident"], writes=[bname])
                p.op("scalar", lambda s, g=g, bank=bank, hqf=hqf: s.copy(
                    out=hqf[:, g, :], in_=bank[:]),
                    reads=[bname], writes=[B_hqf])
            for half in range(2):
                bank = psd[half]
                bname = f"psd{half}"

                def mm_d(t, half=half, bank=bank, hqf=hqf):
                    r = None
                    for k in range(IFF // 128):
                        r = t.matmul(bank[:, 0:512], hqf[:, k, :],
                                     wdt[:, k, half * 512:(half + 1) * 512],
                                     start=(k == 0),
                                     stop=(k == IFF // 128 - 1))
                    return r
                p.op("tensor", mm_d, reads=[B_hqf, "wdt"], writes=[bname])
                p.op("vector", lambda v, half=half, bank=bank, o_sb=o_sb,
                     x1t=x1t: v.tensor_tensor(
                         out=o_sb[:, half * 512:(half + 1) * 512],
                         in0=bank[:, 0:512],
                         in1=x1t[:, half * 512:(half + 1) * 512], op=OP.add),
                     reads=[bname, B_x1], writes=[B_o])
            p.op("vector", lambda v, o_sb=o_sb: v.tensor_reduce(
                out=muC[:], in_=o_sb[:], axis=AX.X, op=OP.add),
                reads=[B_o], writes=["mu"])
            p.op("scalar", lambda s, o_sb=o_sb, pp=pp: s.activation(
                out=scrh2[pp]["t1"][:, 0:DM], in_=o_sb[:], func=AF.Square,
                accum_out=s2C[:]),
                reads=[B_o], writes=[f"qh{pp}_t1", "s2c"])
            p.op("vector", lambda v: v.tensor_scalar(
                out=muC[:], in0=muC[:], scalar1=1.0 / DM, scalar2=None,
                op0=OP.mult), reads=["mu"], writes=["mu"])
            p.op("vector", lambda v: v.tensor_scalar(
                out=varC[:], in0=s2C[:], scalar1=1.0 / DM, scalar2=None,
                op0=OP.mult), reads=["s2c"], writes=["var"])
            p.op("vector", lambda v: v.tensor_tensor(
                out=rs_b2[:], in0=muC[:], in1=muC[:], op=OP.mult),
                reads=["mu"], writes=["rs_b"])
            p.op("vector", lambda v: v.tensor_tensor(
                out=varC[:], in0=varC[:], in1=rs_b2[:], op=OP.subtract),
                reads=["var", "rs_b"], writes=["var"])
            emit_rsqrt(p, rstdC, "rs_d", varC[:], "var", 1e-5, rs_a2, rs_b2,
                       128)
            p.op("vector", lambda v, o_sb=o_sb: v.tensor_scalar(
                out=o_sb[:], in0=o_sb[:], scalar1=muC[:, 0:1],
                scalar2=rstdC[:, 0:1], op0=OP.subtract, op1=OP.mult),
                reads=[B_o], imm_reads=["mu", "rs_d"], writes=[B_o])
            p.dma("scalar", d_out[r0:r0 + 128, :], o_sb[:],
                  reads=[B_o], writes=["out_d"])
        p.drain_all()
        p.flush()

    es.close()
    return nc


_NC_CACHE = None
LAST_RESULTS = None


def _get_nc():
    global _NC_CACHE
    if _NC_CACHE is None:
        _NC_CACHE = build()
    return _NC_CACHE


def kernel(L0_attn_x, L0_attn_in_proj_w, L0_attn_conv_w, L0_attn_conv_b,
           L0_attn_x_proj_w, L0_attn_dt_proj_w, L0_attn_dt_proj_b,
           L0_attn_A_log, L0_attn_Dp, L0_attn_out_proj_w,
           L0_ffn_w_gate, L0_ffn_w_up, L0_ffn_w_down):
    x = np.asarray(L0_attn_x, np.float32)
    nc = _get_nc()
    shared = {
        "wi": np.ascontiguousarray(L0_attn_in_proj_w, np.float32),
        "cw": np.ascontiguousarray(
            L0_attn_conv_w[:, 0, :].reshape(NJU, 128, CONV)
            .transpose(1, 0, 2), np.float32),
        "cb": np.ascontiguousarray(
            L0_attn_conv_b.reshape(NJU, 128).T, np.float32),
        "wx": np.ascontiguousarray(L0_attn_x_proj_w, np.float32),
        "wd": np.ascontiguousarray(L0_attn_dt_proj_w, np.float32),
        "bd": np.ascontiguousarray(
            L0_attn_dt_proj_b.reshape(NJU, 128).T, np.float32),
        "dp": np.ascontiguousarray(L0_attn_Dp.reshape(NJU, 128).T, np.float32),
        "wo": np.ascontiguousarray(L0_attn_out_proj_w, np.float32),
    }
    in_maps = []
    for c in range(NCORE):
        b, h = c // 2, c % 2
        t0 = h * TLOC
        xT = x[b].T
        xfm = np.zeros((DM, TLOC + 3), np.float32)
        if t0 >= 3:
            xfm[:, 0:3] = xT[:, t0 - 3:t0]
        xfm[:, 3:] = xT[:, t0:t0 + TLOC]
        m = dict(shared)
        m["x_fm"] = np.ascontiguousarray(xfm)
        m["x_tok"] = np.ascontiguousarray(x[b, t0:t0 + TLOC])
        m["alogs"] = np.ascontiguousarray(
            L0_attn_A_log[h * (DI // 2):(h + 1) * (DI // 2)]
            .reshape(NJB, 128, NST).transpose(1, 0, 2), np.float32)
        m["dhv"] = np.array([[h]], np.int32)
        m["wgsh"] = np.ascontiguousarray(
            L0_ffn_w_gate[c * (IFF // 8):(c + 1) * (IFF // 8)], np.float32)
        m["wush"] = np.ascontiguousarray(
            L0_ffn_w_up[c * (IFF // 8):(c + 1) * (IFF // 8)], np.float32)
        m["wdsh"] = np.ascontiguousarray(
            L0_ffn_w_down[c * (DM // 8):(c + 1) * (DM // 8)], np.float32)
        in_maps.append(m)

    res = run_bass_kernel_spmd(nc, in_maps, core_ids=list(range(NCORE)))
    global LAST_RESULTS
    LAST_RESULTS = res
    out = np.empty((B, L, DM), np.float32)
    for c in range(NCORE):
        b, h = c // 2, c % 2
        out[b, h * TLOC:(h + 1) * TLOC] = res.results[c]["out"]
    return out


# version 34
# speedup vs baseline: 1.0705x; 1.0705x over previous
import contextlib
import numpy as np
import ml_dtypes
import concourse.bass as bass
import concourse.mybir as mybir
from concourse.bass_utils import run_bass_kernel_spmd

F32 = mybir.dt.float32
BF16 = mybir.dt.bfloat16
I32 = mybir.dt.int32
AF = mybir.ActivationFunctionType
OP = mybir.AluOpType
AX = mybir.AxisListType
NPBF = ml_dtypes.bfloat16

B, L, DM = 4, 8192, 1024
DI, NST, DTR, CONV = 2048, 16, 64, 4
IFF = 2816
TLOC = 4096
NCORE = 8
TAU = 512
TC = 1024
NJU = DI // 128
NJI = 2 * DI // 128
NJB = 8
NKD = DM // 128
PAIRS = [[0, 1], [2, 3], [4, 5], [6, 7]]
ALLG = [list(range(8))]
MAGIC = 6291456.0

ENGS = ("sync", "vector", "scalar", "tensor", "gpsimd")
ESEM = {"sync": "ld", "vector": "dve", "scalar": "act", "tensor": "pe",
        "gpsimd": "gp"}
DEBUG_TAPS = False
BUILD_PHASES = "ALL"


_SKIP_REST = [False]


class Prog:

    NDQ = 8

    def __init__(self, nc, es):
        self.nc = nc
        self.es = es
        self.sems = {}
        self.counts = {}
        self.ops = {e: [] for e in ENGS}
        self.last_w = {}
        self.readers = {}
        self.hwm = {e: {} for e in ENGS}
        self.fam = {}
        self.fam_next = 0
        self.skip_vv = False

    def sem(self, name):
        if name not in self.sems:
            self.sems[name] = self.es.enter_context(self.nc.semaphore(name))
            self.counts.setdefault(name, 0)
        return self.sems[name]

    def val(self, name):
        return self.counts.get(name, 0)

    def _family(self, buf):
        if buf not in self.fam:
            self.fam[buf] = self.fam_next % self.NDQ
            self.fam_next += 1
        return self.fam[buf]

    def op(self, eng, fn, reads=(), writes=(), imm_reads=(), dma=False,
           extra_waits=(), sem_override=None, inc_n=None):
        if sem_override is not None:
            semn = sem_override
        elif dma:
            key = writes[0] if writes else (reads[0] if reads else "misc")
            semn = f"dq{self._family(key)}"
        else:
            semn = ESEM[eng]
        self.sem(semn)
        n = inc_n if inc_n is not None else (16 if dma else 1)

        deps = {}

        def add(tok, force=False):
            if tok is None:
                return
            kind, s, v, peng = tok
            if kind == "dma" or kind == "cc":
                v = self.counts[s]
            elif not force and not dma and peng == eng and (
                    eng == "tensor" or (eng == "vector" and self.skip_vv)):
                return
            if deps.get(s, -1) < v:
                deps[s] = v

        for b in reads:
            add(self.last_w.get(b))
        for b in imm_reads:
            add(self.last_w.get(b), force=True)
        for b in writes:
            add(self.last_w.get(b))
            for tok in self.readers.get(b, ()):
                add(tok)
        for s, v in extra_waits:
            self.sem(s)
            if deps.get(s, -1) < v:
                deps[s] = v

        waits = []
        for s, v in deps.items():
            if self.hwm[eng].get(s, -1) >= v:
                continue
            self.hwm[eng][s] = v
            waits.append((s, v))

        self.counts[semn] += n
        kind = "dma" if dma else ("cc" if sem_override and
                                  sem_override.startswith("cc") else "eng")
        tok = (kind, semn, self.counts[semn], eng)
        for b in writes:
            self.last_w[b] = tok
            self.readers[b] = []
        for b in list(reads) + list(imm_reads):
            self.readers.setdefault(b, []).append(tok)
        self.ops[eng].append((waits, fn, self.sems[semn], n))
        return tok

    def drain_all(self):
        snap = [(s, v) for s, v in self.counts.items()
                if not s.startswith("cc")]
        self.op("gpsimd", lambda g: None, extra_waits=snap, inc_n=1)

    def fence(self):
        snap = [(s, v) for s, v in self.counts.items()
                if not s.startswith("cc")]
        for e in ENGS:
            self.op(e, lambda _e: None, extra_waits=snap, inc_n=1)

    def dma(self, eng, out, in_, reads=(), writes=(), **kw):
        def f(e, out=out, in_=in_):
            return e.dma_start(out=out, in_=in_)
        return self.op(eng, f, reads=reads, writes=writes, dma=True, **kw)

    def flush(self):
        ops = self.ops
        self.ops = {e: [] for e in ENGS}
        with self.nc.Block() as block:
            for ename in ENGS:
                if not ops[ename]:
                    continue

                def body(eng, _ops=ops[ename]):
                    for waits, fn, inc_sem, inc_n in _ops:
                        for s, v in waits:
                            eng.wait_ge(self.sems[s], v)
                        instr = fn(eng)
                        if instr is None:
                            eng.sem_inc(inc_sem, inc_n)
                        else:
                            instr.then_inc(inc_sem, inc_n)
                getattr(block, ename)(body)


def emit_quant(p, src, dst_bf, pdim, fdim, scr, src_buf, dst_buf,
               extra_waits=(), pfx="q", mcols=None, bs_pfx=None):
    nb = fdim // 16
    vv, av, t1, b2, s16 = scr["vv"], scr["av"], scr["t1"], scr["b2"], scr["s16"]
    sg = b2
    bm, sc = scr["bm"], scr["sc"]
    bs = bs_pfx or pfx

    p.op("vector", lambda v: v.tensor_reduce(
        out=bm[0:pdim, 0:nb], in_=src.rearrange("p (b s) -> p b s", s=16),
        axis=AX.X, op=OP.max, apply_absolute_value=True),
        reads=[src_buf], writes=[bs + "_bm"], extra_waits=extra_waits)
    p.op("vector", lambda v: v.tensor_scalar(
        out=sc[0:pdim, 0:nb], in0=bm[0:pdim, 0:nb], scalar1=1.0 / 6.0,
        scalar2=1e-12, op0=OP.mult, op1=OP.max),
        reads=[bs + "_bm"], writes=[bs + "_sc"])
    p.op("vector", lambda v: v.reciprocal(out=bm[0:pdim, 0:nb],
                                          in_=sc[0:pdim, 0:nb]),
         reads=[bs + "_sc"], writes=[bs + "_bm"])
    p.op("vector", lambda v: v.tensor_copy(
        out=s16[0:pdim, 0:fdim].rearrange("p (b s) -> p b s", s=16),
        in_=sc[0:pdim, 0:nb].rearrange("p (b o) -> p b o", o=1).broadcast_to(
            [pdim, nb, 16])), reads=[bs + "_sc"], writes=[bs + "_s16"])
    p.op("vector", lambda v: v.tensor_copy(
        out=vv[0:pdim, 0:fdim].rearrange("p (b s) -> p b s", s=16),
        in_=bm[0:pdim, 0:nb].rearrange("p (b o) -> p b o", o=1).broadcast_to(
            [pdim, nb, 16])), reads=[bs + "_bm"], writes=[pfx + "_vv"])
    p.op("vector", lambda v: v.tensor_tensor(
        out=vv[0:pdim, 0:fdim], in0=src, in1=vv[0:pdim, 0:fdim], op=OP.mult),
        reads=[src_buf, pfx + "_vv"], writes=[pfx + "_vv"])
    avf = av[0:pdim, 0:fdim]
    avi = av.ap().bitcast(I32)[0:pdim, 0:fdim]
    b2i = b2.ap().bitcast(I32)[0:pdim, 0:fdim]
    p.op("scalar", lambda s: s.activation(
        out=avf, in_=vv[0:pdim, 0:fdim], func=AF.Abs),
        reads=[pfx + "_vv"], writes=[pfx + "_av"])

    mp, mn = mcols
    p.op("scalar", lambda s: s.activation(
        out=t1[0:pdim, 0:fdim], in_=avf, func=AF.Identity,
        bias=mp[0:pdim, 0:1]),
        reads=[pfx + "_av"], imm_reads=["mcol"], writes=[pfx + "_t1"])
    p.op("scalar", lambda s: s.activation(
        out=t1[0:pdim, 0:fdim], in_=t1[0:pdim, 0:fdim], func=AF.Identity,
        bias=mn[0:pdim, 0:1]), reads=[pfx + "_t1"], writes=[pfx + "_t1"])
    p.op("vector", lambda v: v.tensor_scalar(
        out=b2i, in0=avi, scalar1=0x3FFFFF, scalar2=None, op0=OP.bitwise_and),
        reads=[pfx + "_av"], writes=[pfx + "_b2"])
    p.op("vector", lambda v: v.tensor_scalar(
        out=b2i, in0=b2i, scalar1=0x200000, scalar2=None, op0=OP.is_gt),
        reads=[pfx + "_b2"], writes=[pfx + "_b2"])
    p.op("vector", lambda v: v.tensor_scalar(
        out=avi, in0=avi, scalar1=22, scalar2=None, op0=OP.arith_shift_right),
        reads=[pfx + "_av"], writes=[pfx + "_av"])
    p.op("vector", lambda v: v.tensor_tensor(
        out=avi, in0=avi, in1=b2i, op=OP.add),
        reads=[pfx + "_av", pfx + "_b2"], writes=[pfx + "_av"])
    p.op("vector", lambda v: v.tensor_scalar(
        out=avi, in0=avi, scalar1=22, scalar2=None,
        op0=OP.logical_shift_left), reads=[pfx + "_av"], writes=[pfx + "_av"])
    p.op("vector", lambda v: v.tensor_scalar(
        out=avf, in0=avf, scalar1=1.0, scalar2=None, op0=OP.max),
        reads=[pfx + "_av"], writes=[pfx + "_av"])
    p.op("scalar", lambda s: s.activation(
        out=sg[0:pdim, 0:fdim], in_=vv[0:pdim, 0:fdim], func=AF.Sign),
        reads=[pfx + "_vv"], writes=[pfx + "_b2"])
    p.op("vector", lambda v: v.scalar_tensor_tensor(
        out=t1[0:pdim, 0:fdim], in0=t1[0:pdim, 0:fdim], scalar=1.0, in1=avf,
        op0=OP.min, op1=OP.add), reads=[pfx + "_t1", pfx + "_av"], writes=[pfx + "_t1"])
    p.op("vector", lambda v: v.scalar_tensor_tensor(
        out=t1[0:pdim, 0:fdim], in0=t1[0:pdim, 0:fdim], scalar=-1.0,
        in1=s16[0:pdim, 0:fdim], op0=OP.add, op1=OP.mult),
        reads=[pfx + "_t1", bs + "_s16"], writes=[pfx + "_t1"])
    p.op("vector", lambda v: v.tensor_tensor(
        out=dst_bf, in0=t1[0:pdim, 0:fdim], in1=sg[0:pdim, 0:fdim],
        op=OP.mult), reads=[pfx + "_t1", pfx + "_b2"], writes=[dst_buf])


def emit_rsqrt(p, dst, dst_buf, var_ap, var_buf, eps, sa, sb, pdim):
    di = dst.ap().bitcast(I32)[0:pdim, 0:1]
    ai = sa.ap().bitcast(I32)[0:pdim, 0:1]
    p.op("vector", lambda v: v.tensor_scalar(
        out=sa[0:pdim, 0:1], in0=var_ap, scalar1=eps, scalar2=None,
        op0=OP.add), reads=[var_buf], writes=["rs_a"])
    p.op("vector", lambda v: v.tensor_scalar(
        out=di, in0=ai, scalar1=1, scalar2=None, op0=OP.arith_shift_right),
        reads=["rs_a"], writes=["rs_d"])
    p.op("vector", lambda v: v.tensor_scalar(
        out=di, in0=di, scalar1=0x5F370000, scalar2=None, op0=OP.subtract),
        reads=["rs_d"], writes=["rs_d"])
    p.op("vector", lambda v: v.tensor_scalar(
        out=di, in0=di, scalar1=0, scalar2=None, op0=OP.bitwise_not),
        reads=["rs_d"], writes=["rs_d"])
    p.op("vector", lambda v: v.tensor_scalar(
        out=di, in0=di, scalar1=1, scalar2=None, op0=OP.add),
        reads=["rs_d"], writes=["rs_d"])
    for _ in range(3):
        p.op("vector", lambda v: v.tensor_tensor(
            out=sb[0:pdim, 0:1], in0=dst[0:pdim, 0:1], in1=dst[0:pdim, 0:1],
            op=OP.mult), reads=["rs_d"], writes=["rs_b"])
        p.op("vector", lambda v: v.tensor_tensor(
            out=sb[0:pdim, 0:1], in0=sb[0:pdim, 0:1], in1=sa[0:pdim, 0:1],
            op=OP.mult), reads=["rs_b", "rs_a"], writes=["rs_b"])
        p.op("vector", lambda v: v.tensor_scalar(
            out=sb[0:pdim, 0:1], in0=sb[0:pdim, 0:1], scalar1=-0.5,
            scalar2=1.5, op0=OP.mult, op1=OP.add),
            reads=["rs_b"], writes=["rs_b"])
        p.op("vector", lambda v: v.tensor_tensor(
            out=dst[0:pdim, 0:1], in0=dst[0:pdim, 0:1], in1=sb[0:pdim, 0:1],
            op=OP.mult), reads=["rs_d", "rs_b"], writes=["rs_d"])


def build():
    nc = bass.Bass("TRN2", target_bir_lowering=False)
    es = contextlib.ExitStack()
    p = Prog(nc, es)

    d_xfm = nc.dram_tensor("x_fm", [DM, TLOC + 3], F32, kind="ExternalInput").ap()
    d_xtok = nc.dram_tensor("x_tok", [TLOC, DM], F32, kind="ExternalInput").ap()
    d_wi = nc.dram_tensor("wi", [DM, 2 * DI], F32, kind="ExternalInput").ap()
    d_cw = nc.dram_tensor("cw", [128, NJU, CONV], F32, kind="ExternalInput").ap()
    d_cb = nc.dram_tensor("cb", [128, NJU], F32, kind="ExternalInput").ap()
    d_wx = nc.dram_tensor("wx", [DI, 96], F32, kind="ExternalInput").ap()
    d_wd = nc.dram_tensor("wd", [DTR, DI], F32, kind="ExternalInput").ap()
    d_bd = nc.dram_tensor("bd", [128, NJU], F32, kind="ExternalInput").ap()
    d_alogs = nc.dram_tensor("alogs", [128, NJB, NST], F32, kind="ExternalInput").ap()
    d_dp = nc.dram_tensor("dp", [128, NJU], F32, kind="ExternalInput").ap()
    d_wo = nc.dram_tensor("wo", [DI, DM], F32, kind="ExternalInput").ap()
    d_wgsh = nc.dram_tensor("wgsh", [IFF // 8, DM], F32, kind="ExternalInput").ap()
    d_wush = nc.dram_tensor("wush", [IFF // 8, DM], F32, kind="ExternalInput").ap()
    d_wdsh = nc.dram_tensor("wdsh", [DM // 8, IFF], F32, kind="ExternalInput").ap()
    d_dhv = nc.dram_tensor("dhv", [1, 1], I32, kind="ExternalInput").ap()
    d_out = nc.dram_tensor("out", [TLOC, DM], F32, kind="ExternalOutput").ap()

    ag_uc_in = nc.dram_tensor("ag_uc_in", [DI, TLOC], BF16)
    ag_uc_out = nc.dram_tensor("ag_uc_out", [2 * DI, TLOC], BF16)
    ag_dt_in = nc.dram_tensor("ag_dt_in", [DI, TLOC], BF16)
    ag_dt_out = nc.dram_tensor("ag_dt_out", [2 * DI, TLOC], BF16)
    ag_bc_in = nc.dram_tensor("ag_bc_in", [32, TLOC], BF16)
    ag_bc_out = nc.dram_tensor("ag_bc_out", [64, TLOC], BF16)
    ag_ys_in = nc.dram_tensor("ag_ys_in", [DI // 2, L], BF16)
    ag_ys_out = nc.dram_tensor("ag_ys_out", [DI, L], BF16)
    ucv = nc.dram_tensor("ucv", [DI // 2, L], BF16)
    dtv = nc.dram_tensor("dtv", [DI // 2, L], BF16)
    sz_priv = nc.dram_tensor("sz_priv", [DI, TLOC], BF16)
    x1_priv = nc.dram_tensor("x1_priv", [TLOC, DM], F32)
    wgq_in = nc.dram_tensor("wgq_in", [IFF // 8, DM], BF16)
    wgq_out = nc.dram_tensor("wgq_out", [IFF, DM], BF16, addr_space="Shared")
    wuq_in = nc.dram_tensor("wuq_in", [IFF // 8, DM], BF16)
    wuq_out = nc.dram_tensor("wuq_out", [IFF, DM], BF16, addr_space="Shared")
    wdq_in = nc.dram_tensor("wdq_in", [DM // 8, IFF], BF16)
    wdq_out = nc.dram_tensor("wdq_out", [DM, IFF], BF16, addr_space="Shared")

    taps = {}
    if DEBUG_TAPS:
        taps["t_uc"] = nc.dram_tensor("t_uc", [2 * DI, TLOC], BF16,
                                      kind="ExternalOutput").ap()
        taps["t_dt"] = nc.dram_tensor("t_dt", [2 * DI, TLOC], BF16,
                                      kind="ExternalOutput").ap()
        taps["t_bc"] = nc.dram_tensor("t_bc", [64, TLOC], BF16,
                                      kind="ExternalOutput").ap()
        taps["t_ys"] = nc.dram_tensor("t_ys", [DI, L], BF16,
                                      kind="ExternalOutput").ap()
        taps["t_x1"] = nc.dram_tensor("t_x1", [TLOC, DM], F32,
                                      kind="ExternalOutput").ap()

    A_sb = es.enter_context(nc.sbuf_tensor("A_sb", [128, NJB, NST], F32))
    dp_sb = es.enter_context(nc.sbuf_tensor("dp_sb", [128, NJU], F32))
    ident = es.enter_context(nc.sbuf_tensor("ident", [128, 128], BF16))
    dh_sb = es.enter_context(nc.sbuf_tensor("dh_sb", [1, 1], I32))
    mcp = es.enter_context(nc.sbuf_tensor("mcp", [128, 1], F32))
    mcn = es.enter_context(nc.sbuf_tensor("mcn", [128, 1], F32))
    rdh = es.enter_context(nc.gpsimd.register("rdh"))
    rt1 = es.enter_context(nc.gpsimd.register("rt1"))
    rt2 = es.enter_context(nc.gpsimd.register("rt2"))

    with contextlib.ExitStack() as es0:
        wstg = es0.enter_context(nc.sbuf_tensor("wstg", [128, IFF], F32))
        wqo = es0.enter_context(nc.sbuf_tensor("wqo", [128, IFF], BF16))
        scr0 = {k: es0.enter_context(nc.sbuf_tensor(f"q0_{k}", [128, IFF], F32))
                for k in ("vv", "av", "t1", "b2", "s16")}
        scr0["bm"] = es0.enter_context(nc.sbuf_tensor("q0_bm", [128, IFF // 16], F32))
        scr0["sc"] = es0.enter_context(nc.sbuf_tensor("q0_sc", [128, IFF // 16], F32))
        ones0 = es0.enter_context(nc.sbuf_tensor("ones0", [128, 128], F32))
        alog_sb = es0.enter_context(nc.sbuf_tensor("alog_sb", [128, NJB, NST], F32))

        p.dma("sync", dh_sb[:], d_dhv, writes=["dh_sb"])
        p.op("gpsimd", lambda g: g.reg_load(rdh, dh_sb[0:1, 0:1]),
             imm_reads=["dh_sb"])
        p.dma("sync", alog_sb[:], d_alogs, writes=["alog_sb"])
        p.dma("sync", dp_sb[:], d_dp, writes=["dp_sb"])
        p.op("scalar", lambda s: s.activation(
            out=A_sb.ap().rearrange("p j n -> p (j n)"),
            in_=alog_sb.ap().rearrange("p j n -> p (j n)"), func=AF.Exp),
            reads=["alog_sb"], writes=["A_sb"])
        p.op("vector", lambda v: v.tensor_scalar(
            out=A_sb.ap().rearrange("p j n -> p (j n)"),
            in0=A_sb.ap().rearrange("p j n -> p (j n)"), scalar1=-1.0,
            scalar2=None, op0=OP.mult), reads=["A_sb"], writes=["A_sb"])
        p.op("gpsimd", lambda g: g.memset(ones0[:], 1.0), writes=["ones0"])
        p.op("vector", lambda v: v.memset(mcp[:], MAGIC), writes=["mcol"])
        p.op("vector", lambda v: v.memset(mcn[:], -MAGIC), writes=["mcol"])
        p.op("gpsimd", lambda g: g.affine_select(
            out=ident[:], in_=ones0[:], pattern=[[1, 128]],
            compare_op=OP.is_equal, fill=0.0, base=0, channel_multiplier=-1),
            reads=["ones0"], writes=["ident"])

        def quant_rows(src_d, dst_d, nrows, fdim):
            r0 = 0
            while r0 < nrows:
                pr = min(128, nrows - r0)
                p.dma("sync", wstg[0:pr, 0:fdim], src_d[r0:r0 + pr, 0:fdim],
                      writes=["wstg"])
                emit_quant(p, wstg[0:pr, 0:fdim], wqo[0:pr, 0:fdim], pr, fdim,
                           scr0, "wstg", "wqo", mcols=(mcp, mcn))
                p.dma("sync", dst_d[r0:r0 + pr, 0:fdim], wqo[0:pr, 0:fdim],
                      reads=["wqo"], writes=[dst_d.tensor.name + "_d"])
                r0 += pr

        quant_rows(d_wgsh, wgq_in.ap(), IFF // 8, DM)
        quant_rows(d_wush, wuq_in.ap(), IFF // 8, DM)
        quant_rows(d_wdsh, wdq_in.ap(), DM // 8, IFF)

        p.op("gpsimd", lambda g: g.collective_compute(
            "AllGather", OP.bypass, replica_groups=ALLG,
            ins=[wgq_in.ap().opt()], outs=[wgq_out.ap().opt()]),
            reads=["wgq_in_d"], writes=["wgq_out_d"], sem_override="ccw0",
            inc_n=1)
        p.op("gpsimd", lambda g: g.collective_compute(
            "AllGather", OP.bypass, replica_groups=ALLG,
            ins=[wuq_in.ap().opt()], outs=[wuq_out.ap().opt()]),
            reads=["wuq_in_d"], writes=["wuq_out_d"], sem_override="ccw1",
            inc_n=1)
        p.op("gpsimd", lambda g: g.collective_compute(
            "AllGather", OP.bypass, replica_groups=ALLG,
            ins=[wdq_in.ap().opt()], outs=[wdq_out.ap().opt()]),
            reads=["wdq_in_d"], writes=["wdq_out_d"], sem_override="ccw2",
            inc_n=1)
        p.flush()

    if BUILD_PHASES == "0":
        p.drain_all()
        p.flush()
        es.close()
        return nc

    p.fence()
    with contextlib.ExitStack() as esA:
        wi_sb = esA.enter_context(nc.sbuf_tensor("wi_sb", [128, NKD, 2 * DI], BF16))
        wx_sb = esA.enter_context(nc.sbuf_tensor("wx_sb", [128, NJU, 96], BF16))
        wd_sb = esA.enter_context(nc.sbuf_tensor("wd_sb", [64, DI], BF16))
        diag = esA.enter_context(nc.sbuf_tensor("diag", [128, NJU, CONV, 128], BF16))
        cb_sb = esA.enter_context(nc.sbuf_tensor("cb_sb", [128, NJU], F32))
        bd_sb = esA.enter_context(nc.sbuf_tensor("bd_sb", [128, NJU], F32))
        cw_sb = esA.enter_context(nc.sbuf_tensor("cw_sb", [128, NJU, CONV], F32))
        wstgA2 = [esA.enter_context(nc.sbuf_tensor(f"wstgA{i}", [128, DI], F32))
                  for i in range(2)]
        onesA = esA.enter_context(nc.sbuf_tensor("onesA", [128, 128], F32))
        xst = esA.enter_context(nc.sbuf_tensor("xst", [128, NKD, TAU + 3], F32))
        xbf1 = esA.enter_context(nc.sbuf_tensor("xbf0", [128, NKD, TAU + 3], BF16))
        xbf2 = [xbf1, xbf1]
        u_sb = esA.enter_context(nc.sbuf_tensor("u_sb", [128, NJU, TAU + 3], BF16))
        uc_sb = esA.enter_context(nc.sbuf_tensor("uc_sb", [128, NJU, TAU], BF16))
        sz_sb = esA.enter_context(nc.sbuf_tensor("sz_sb", [128, NJU, TAU], BF16))
        dtlo_sb = esA.enter_context(nc.sbuf_tensor("dtlo_sb", [64, TAU], BF16))
        bc_sb = esA.enter_context(nc.sbuf_tensor("bc_sb", [32, TAU], BF16))
        dt_sb = esA.enter_context(nc.sbuf_tensor("dt_sb", [128, NJU, TAU], BF16))
        v_sb1 = esA.enter_context(nc.sbuf_tensor("v_sb0", [128, TAU], F32))
        v_sb2 = [v_sb1, v_sb1]
        va_sb1 = esA.enter_context(nc.sbuf_tensor("va_sb0", [128, TAU], F32))
        va_sb2 = [va_sb1, va_sb1]
        e_sb1 = esA.enter_context(nc.sbuf_tensor("e_sb0", [128, TAU], F32))
        e_sb2 = [e_sb1, e_sb1]
        l_sb1 = esA.enter_context(nc.sbuf_tensor("l_sb0", [128, TAU], F32))
        l_sb2 = [l_sb1, l_sb1]
        r_sb1 = esA.enter_context(nc.sbuf_tensor("r_sb0", [128, TAU], F32))
        r_sb2 = [r_sb1, r_sb1]
        psu = [esA.enter_context(nc.psum_tensor(f"psu{i}", [128, TAU], F32))
               for i in range(4)]
        pscv = [esA.enter_context(nc.psum_tensor(f"pscv{i}", [128, TAU], F32))
                for i in range(2)]
        ps_x = esA.enter_context(nc.psum_tensor("ps_x", [96, TAU], F32))

        for k in range(NKD):
            for hh in range(2):
                wp = (2 * k + hh) % 2
                wstgA = wstgA2[wp]
                p.dma("sync", wstgA[:, 0:DI],
                      d_wi[k * 128:(k + 1) * 128, hh * DI:(hh + 1) * DI],
                      writes=[f"wstgA{wp}"])
                p.op("vector", lambda v, k=k, hh=hh, wstgA=wstgA:
                     v.tensor_copy(
                         out=wi_sb[:, k, hh * DI:(hh + 1) * DI],
                         in_=wstgA[:, 0:DI]),
                     reads=[f"wstgA{wp}"], writes=["wi_sb"])
        p.dma("sync", wstgA2[0][:, 0:NJU * 96].rearrange("p (j n) -> p j n", n=96),
              d_wx.rearrange("(j p) n -> p j n", p=128), writes=["wstgA0"])
        p.op("vector", lambda v: v.tensor_copy(
            out=wx_sb.ap().rearrange("p j n -> p (j n)"),
            in_=wstgA2[0][:, 0:NJU * 96]), reads=["wstgA0"], writes=["wx_sb"])
        p.dma("sync", wstgA2[1][0:64, 0:DI], d_wd, writes=["wstgA1"])
        p.op("vector", lambda v: v.tensor_copy(
            out=wd_sb[:], in_=wstgA2[1][0:64, 0:DI]),
            reads=["wstgA1"], writes=["wd_sb"])
        p.dma("sync", cw_sb[:], d_cw, writes=["cw_sb"])
        p.dma("sync", cb_sb[:], d_cb, writes=["cb_sb"])
        p.dma("sync", bd_sb[:], d_bd, writes=["bd_sb"])
        p.op("gpsimd", lambda g: g.memset(onesA[:], 1.0), writes=["onesA"])
        for j in range(NJU):
            for k in range(CONV):
                p.op("gpsimd", lambda g, j=j, k=k: g.tensor_scalar(
                    out=diag[:, j, k, :], in0=onesA[:],
                    scalar1=cw_sb[:, j, k:k + 1], scalar2=None, op0=OP.mult),
                    reads=["onesA"], imm_reads=["cw_sb"], writes=["diag"])
        for j in range(NJU):
            for k in range(CONV):
                p.op("gpsimd", lambda g, j=j, k=k: g.affine_select(
                    out=diag[:, j, k, :], in_=diag[:, j, k, :],
                    pattern=[[1, 128]], compare_op=OP.is_equal, fill=0.0,
                    base=0, channel_multiplier=-1),
                    reads=["diag"], writes=["diag"])

        if BUILD_PHASES == "A1":
            p.drain_all()
        NT = 0 if BUILD_PHASES == "A1" else TLOC // TAU
        for i in range(min(NT, 1) if BUILD_PHASES == "A2" else NT):
            c0 = i * TAU
            ip = i % 2
            xbf = xbf2[ip]
            B_xst, B_xbf = "xst", "xbf0"
            p.dma("sync", xst[:], d_xfm.rearrange("(k p) t -> p k t", p=128)
                  [:, :, c0:c0 + TAU + 3], writes=[B_xst])
            p.op("vector", lambda v, xbf=xbf, xst=xst: v.tensor_copy(
                out=xbf[:], in_=xst[:]), reads=[B_xst], writes=[B_xbf])
            if i > 0:
                p.op("vector", lambda v: v.tensor_copy(
                    out=u_sb[:, :, 0:3], in_=u_sb[:, :, TAU:TAU + 3]),
                    reads=["u_sb"], writes=["u_sb"])
            for j in range(NJI):
                is_u = j < NJU
                bank = psu[j % 4]
                bname = f"psu{j % 4}"

                def mm_ip(t, j=j, bank=bank, xbf=xbf):
                    r = None
                    for k in range(NKD):
                        r = t.matmul(bank[:, 0:TAU],
                                     wi_sb[:, k, j * 128:(j + 1) * 128],
                                     xbf[:, k, 3:3 + TAU],
                                     start=(k == 0), stop=(k == NKD - 1))
                    return r
                p.op("tensor", mm_ip, reads=["wi_sb", B_xbf], writes=[bname])
                if is_u:
                    p.op("vector", lambda v, j=j, bank=bank: v.tensor_copy(
                        out=u_sb[:, j, 3:3 + TAU], in_=bank[:, 0:TAU]),
                        reads=[bname], writes=["u_sb"])
                else:
                    p.op("scalar", lambda s, j=j, bank=bank: s.activation(
                        out=sz_sb[:, j - NJU, :], in_=bank[:, 0:TAU],
                        func=AF.Silu), reads=[bname], writes=["sz_sb"])
            if i == 0:
                for j in range(NJU):
                    bank = psu[j % 4]
                    bname = f"psu{j % 4}"

                    def mm_h(t, j=j, bank=bank, xbf=xbf):
                        r = None
                        for k in range(NKD):
                            r = t.matmul(bank[:, 0:3],
                                         wi_sb[:, k, j * 128:(j + 1) * 128],
                                         xbf[:, k, 0:3],
                                         start=(k == 0), stop=(k == NKD - 1))
                        return r
                    p.op("tensor", mm_h, reads=["wi_sb", B_xbf],
                         writes=[bname])
                    p.op("vector", lambda v, j=j, bank=bank: v.tensor_copy(
                        out=u_sb[:, j, 0:3], in_=bank[:, 0:3]),
                        reads=[bname], writes=["u_sb"])
            for jc in range(0, NJU, 4):
                p.dma("scalar", sz_priv.ap()
                      .rearrange("(j p) t -> p j t", p=128)
                      [:, jc:jc + 4, c0:c0 + TAU], sz_sb[:, jc:jc + 4, :],
                      reads=["sz_sb"], writes=["sz_priv_d"])
            for j in range(NJU):
                bank = pscv[j % 2]
                bname = f"pscv{j % 2}"

                def mm_cv(t, j=j, bank=bank):
                    r = None
                    for k in range(CONV):
                        r = t.matmul(bank[:, 0:TAU], diag[:, j, k, :],
                                     u_sb[:, j, k:k + TAU],
                                     start=(k == 0), stop=(k == CONV - 1))
                    return r
                p.op("tensor", mm_cv, reads=["diag", "u_sb"], writes=[bname])
                p.op("scalar", lambda s, j=j, bank=bank: s.activation(
                    out=uc_sb[:, j, :], in_=bank[:, 0:TAU], func=AF.Silu,
                    bias=cb_sb[:, j:j + 1]), reads=[bname],
                    imm_reads=["cb_sb"], writes=["uc_sb"])
            for jc in range(0, NJU, 4):
                p.dma("scalar", ag_uc_in.ap()
                      .rearrange("(j p) t -> p j t", p=128)
                      [:, jc:jc + 4, c0:c0 + TAU], uc_sb[:, jc:jc + 4, :],
                      reads=["uc_sb"], writes=["ag_uc_in_d"])

            def mm_xp(t):
                r = None
                for k in range(NJU):
                    r = t.matmul(ps_x[:, 0:TAU], wx_sb[:, k, :],
                                 uc_sb[:, k, :],
                                 start=(k == 0), stop=(k == NJU - 1))
                return r
            p.op("tensor", mm_xp, reads=["wx_sb", "uc_sb"], writes=["ps_x"])
            p.op("vector", lambda v: v.tensor_copy(
                out=dtlo_sb[:], in_=ps_x[0:64, 0:TAU]),
                reads=["ps_x"], writes=["dtlo_sb"])
            p.op("vector", lambda v: v.tensor_copy(
                out=bc_sb[:], in_=ps_x[64:96, 0:TAU]),
                reads=["ps_x"], writes=["bc_sb"])
            p.dma("scalar", ag_bc_in.ap()[0:32, c0:c0 + TAU], bc_sb[:],
                  reads=["bc_sb"], writes=["ag_bc_in_d"])
            def sp_front(j):
                jp = j % 2
                bank = psu[jp]
                bname = f"psu{jp}"
                v_sb, va_sb = v_sb2[jp], va_sb2[jp]
                p.op("tensor", lambda t, j=j, bank=bank: t.matmul(
                    bank[:, 0:TAU], wd_sb[:, j * 128:(j + 1) * 128],
                    dtlo_sb[:], start=True, stop=True),
                    reads=["wd_sb", "dtlo_sb"], writes=[bname])
                p.op("vector", lambda v, j=j, bank=bank, v_sb=v_sb:
                     v.tensor_scalar(
                         out=v_sb[:], in0=bank[:, 0:TAU],
                         scalar1=bd_sb[:, j:j + 1], scalar2=None, op0=OP.add),
                     reads=[bname], imm_reads=["bd_sb"], writes=["v_sb0"])
                p.op("vector", lambda v, v_sb=v_sb, va_sb=va_sb:
                     v.tensor_scalar(
                         out=va_sb.ap().bitcast(I32)[:, :],
                         in0=v_sb.ap().bitcast(I32)[:, :],
                         scalar1=0x7FFFFFFF, scalar2=None,
                         op0=OP.bitwise_and),
                     reads=["v_sb0"], writes=["va_sb0"])
                p.op("scalar", lambda s, va_sb=va_sb, e_sb=e_sb2[jp]:
                     s.activation(
                         out=e_sb[:], in_=va_sb[:], func=AF.Exp, scale=-1.0),
                     reads=["va_sb0"], writes=["e_sb0"])

            def sp_back(j):
                jp = j % 2
                v_sb, e_sb, l_sb, r_sb = (v_sb2[jp], e_sb2[jp], l_sb2[jp],
                                          r_sb2[jp])
                p.op("scalar", lambda s, e_sb=e_sb, l_sb=l_sb: s.activation(
                    out=l_sb[:], in_=e_sb[:], func=AF.Ln, bias=1.0),
                    reads=["e_sb0"], writes=["l_sb0"])
                p.op("vector", lambda v, v_sb=v_sb, r_sb=r_sb: v.tensor_scalar(
                    out=r_sb[:], in0=v_sb[:], scalar1=0.0, scalar2=None,
                    op0=OP.max), reads=["v_sb0"], writes=["r_sb0"])
                p.op("vector", lambda v, j=j, l_sb=l_sb, r_sb=r_sb:
                     v.tensor_tensor(
                         out=dt_sb[:, j, :], in0=l_sb[:], in1=r_sb[:],
                         op=OP.add),
                     reads=["l_sb0", "r_sb0"], writes=["dt_sb"])

            for j in range(NJU + 1):
                if j > 0:
                    sp_back(j - 1)
                if j < NJU:
                    sp_front(j)
            for jc in range(0, NJU, 4):
                p.dma("scalar", ag_dt_in.ap()
                      .rearrange("(j p) t -> p j t", p=128)
                      [:, jc:jc + 4, c0:c0 + TAU], dt_sb[:, jc:jc + 4, :],
                      reads=["dt_sb"], writes=["ag_dt_in_d"])

        if BUILD_PHASES in ("A1", "A2", "A3"):
            p.drain_all()
            p.flush()
            _SKIP_REST[0] = True
        if not _SKIP_REST[0]:
            p.op("gpsimd", lambda g: g.collective_compute(
                    "AllGather", OP.bypass, replica_groups=PAIRS,
                    ins=[ag_uc_in.ap().opt()], outs=[ag_uc_out.ap().opt()]),
                reads=["ag_uc_in_d"], writes=["ag_uc_out_d"], sem_override="cca0",
                inc_n=1)
            p.op("gpsimd", lambda g: g.collective_compute(
                "AllGather", OP.bypass, replica_groups=PAIRS,
                ins=[ag_dt_in.ap().opt()], outs=[ag_dt_out.ap().opt()]),
                reads=["ag_dt_in_d"], writes=["ag_dt_out_d"], sem_override="cca1",
                inc_n=1)
            p.op("gpsimd", lambda g: g.collective_compute(
                "AllGather", OP.bypass, replica_groups=PAIRS,
                ins=[ag_bc_in.ap().opt()], outs=[ag_bc_out.ap().opt()]),
                reads=["ag_bc_in_d"], writes=["ag_bc_out_d"], sem_override="cca2",
                inc_n=1)
            _rg = [0]
            for (gout, gname, vout, vname) in (
                    (ag_uc_out, "ag_uc_out_d", ucv, "ucv_d"),
                    (ag_dt_out, "ag_dt_out_d", dtv, "dtv_d")):
                for r in range(2):
                    def resl(g, gout=gout, vout=vout, r=r):
                        g.reg_mul(rt1, rdh, 1024 * TLOC)
                        g.reg_add(rt1, rt1, r * DI * TLOC)
                        return g.dma_start(
                            out=vout.ap()[:, r * TLOC:(r + 1) * TLOC],
                            in_=bass.AP(gout, rt1, [[TLOC, 1024], [1, TLOC]]))
                    p.op("gpsimd", resl, reads=[gname], writes=[vname],
                         dma=True, sem_override=f"rg{_rg[0]}")
                    _rg[0] += 1
            if DEBUG_TAPS:
                p.dma("sync", taps["t_uc"], ag_uc_out.ap(),
                      reads=["ag_uc_out_d"], writes=["tap_uc"])
                p.dma("sync", taps["t_dt"], ag_dt_out.ap(),
                      reads=["ag_dt_out_d"], writes=["tap_dt"])
                p.dma("sync", taps["t_bc"], ag_bc_out.ap(),
                      reads=["ag_bc_out_d"], writes=["tap_bc"])
            p.flush()

    if BUILD_PHASES in ("A", "A1", "A2", "A3"):
        p.drain_all()
        p.flush()
        es.close()
        return nc

    wgt = es.enter_context(nc.sbuf_tensor("wgt", [128, NKD, IFF], BF16))

    p.fence()
    with contextlib.ExitStack() as esB:
        bbc = esB.enter_context(nc.sbuf_tensor("bbc", [128, NST, TC], BF16))
        cbc2 = esB.enter_context(nc.sbuf_tensor("cbc2", [128, NST, TC], BF16))
        dt_t2 = [esB.enter_context(nc.sbuf_tensor(f"dt_t{i}", [128, TC], BF16))
                 for i in range(2)]
        uc_t2 = [esB.enter_context(nc.sbuf_tensor(f"uc_t{i}", [128, TC], BF16))
                 for i in range(2)]
        w_t2 = [esB.enter_context(nc.sbuf_tensor(f"w_t{i}", [128, TC], BF16))
                for i in range(2)]
        dA_t2 = [esB.enter_context(nc.sbuf_tensor(f"dA_t{i}", [128, TC], BF16))
                 for i in range(4)]
        X_t2 = [esB.enter_context(nc.sbuf_tensor(f"X_t{i}", [128, TC], BF16))
                for i in range(2)]
        P16 = esB.enter_context(nc.sbuf_tensor("P16", [128, NST, TC], BF16))
        ysb = esB.enter_context(nc.sbuf_tensor("ysb", [128, TC], BF16))
        stc = esB.enter_context(nc.sbuf_tensor("stc", [128, NJB * NST], F32))

        p.skip_vv = True
        p.op("vector", lambda v: v.memset(stc[:], 0.0), writes=["stc"])
        p.op("sync", lambda e: e.dma_start(
            out=wgt.ap(), in_=wgq_out.ap(), transpose=True),
            reads=["wgq_out_d"], writes=["wgt"], dma=True)
        for tc_i in range(L // TC):
            rank = tc_i // (TLOC // TC)
            coff = (tc_i % (TLOC // TC)) * TC
            gcol = rank * TLOC + coff
            col = tc_i * TC
            for n in range(NST):
                p.dma("sync", bbc[:, n, :],
                      ag_bc_out.ap()[rank * 32 + n: rank * 32 + n + 1,
                                     coff:coff + TC].broadcast_to([128, TC]),
                      reads=["ag_bc_out_d"], writes=["bbc"])
                p.dma("sync", cbc2[:, n, :],
                      ag_bc_out.ap()[rank * 32 + NST + n:
                                     rank * 32 + NST + n + 1,
                                     coff:coff + TC].broadcast_to([128, TC]),
                      reads=["ag_bc_out_d"], writes=["cbc2"])
            for j in range(NJB):
                jp = j % 2
                dt_t, uc_t, w_t = dt_t2[jp], uc_t2[jp], w_t2[jp]
                B_dt, B_uc, B_w = f"dt_t{jp}", f"uc_t{jp}", f"w_t{jp}"
                p.dma("sync", dt_t[:], dtv.ap()[j * 128:(j + 1) * 128,
                                                col:col + TC],
                      reads=["dtv_d"], writes=[B_dt])
                p.dma("sync", uc_t[:], ucv.ap()[j * 128:(j + 1) * 128,
                                                col:col + TC],
                      reads=["ucv_d"], writes=[B_uc])
                p.op("vector", lambda v, dt_t=dt_t, uc_t=uc_t, w_t=w_t:
                     v.tensor_tensor(
                         out=w_t[:], in0=dt_t[:], in1=uc_t[:], op=OP.mult),
                     reads=[B_dt, B_uc], writes=[B_w])
                for n in range(NST):
                    npp = n % 2
                    dA_t = dA_t2[n % 4]
                    dA_name = f"dA_t{n % 4}"
                    X_t = X_t2[npp]
                    B_X = f"X_t{npp}"
                    p.op("scalar", lambda s, j=j, n=n, dA_t=dA_t, dt_t=dt_t:
                         s.activation(
                             out=dA_t[:], in_=dt_t[:], func=AF.Exp,
                             scale=A_sb[:, j, n:n + 1]),
                         reads=[B_dt], imm_reads=["A_sb"], writes=[dA_name])
                    p.op("vector", lambda v, n=n, X_t=X_t, w_t=w_t:
                         v.tensor_tensor(
                             out=X_t[:], in0=w_t[:], in1=bbc[:, n, :],
                             op=OP.mult),
                         reads=[B_w, "bbc"], writes=[B_X])
                    cidx = j * NST + n
                    p.op("vector", lambda v, cidx=cidx, dA_t=dA_t, n=n,
                         X_t=X_t: v.tensor_tensor_scan(
                             out=P16[:, n, :], data0=dA_t[:], data1=X_t[:],
                             initial=stc[:, cidx:cidx + 1],
                             op0=OP.mult, op1=OP.add),
                         reads=[dA_name, B_X], imm_reads=["stc"],
                         writes=["P16"])
                    p.op("vector", lambda v, cidx=cidx, n=n: v.tensor_copy(
                        out=stc[:, cidx:cidx + 1],
                        in_=P16[:, n, TC - 1:TC]),
                        reads=["P16"], writes=["stc"])
                p.op("vector", lambda v: v.tensor_tensor(
                    out=P16.ap(), in0=P16.ap(), in1=cbc2.ap(), op=OP.mult),
                    reads=["P16", "cbc2"], writes=["P16"])
                for lev, step in ((1, 2), (2, 4), (3, 8)):
                    ng = NST // (2 ** lev)
                    v0 = P16.ap().rearrange("p (a b) t -> p a b t", b=step)
                    p.op("vector", lambda v, v0=v0, ng=ng, step=step:
                         v.tensor_tensor(
                             out=v0[:, :, 0, :], in0=v0[:, :, 0, :],
                             in1=v0[:, :, step // 2, :], op=OP.add),
                         reads=["P16"], writes=["P16"])
                p.op("vector", lambda v: v.tensor_tensor(
                    out=ysb[:], in0=P16[:, 0, :], in1=P16[:, 8, :],
                    op=OP.add), reads=["P16"], writes=["ysb"])
                p.dma("scalar", ag_ys_in.ap()[j * 128:(j + 1) * 128,
                                              col:col + TC], ysb[:],
                      reads=["ysb"], writes=["ag_ys_in_d"])
        p.skip_vv = False
        p.op("gpsimd", lambda g: g.collective_compute(
            "AllGather", OP.bypass, replica_groups=PAIRS,
            ins=[ag_ys_in.ap().opt()], outs=[ag_ys_out.ap().opt()]),
            reads=["ag_ys_in_d"], writes=["ag_ys_out_d"], sem_override="ccy0",
            inc_n=1)
        if DEBUG_TAPS:
            p.dma("sync", taps["t_ys"], ag_ys_out.ap(),
                  reads=["ag_ys_out_d"], writes=["tap_ys"])
        p.flush()

    if BUILD_PHASES == "AB":
        p.drain_all()
        p.flush()
        es.close()
        return nc

    p.fence()
    with contextlib.ExitStack() as esC1:
        wo_sb = esC1.enter_context(nc.sbuf_tensor("wo_sb", [128, NJU, DM], BF16))
        wstgC = esC1.enter_context(nc.sbuf_tensor("wstgC", [128, DM], F32))
        ys_sb2 = [esC1.enter_context(nc.sbuf_tensor(f"ys_sb{i}", [128, NJU, TAU], BF16))
                  for i in range(2)]
        ucC2 = [esC1.enter_context(nc.sbuf_tensor(f"ucC{i}", [128, NJU, TAU], BF16))
                for i in range(2)]
        szC1a = esC1.enter_context(nc.sbuf_tensor("szC0", [128, NJU, TAU], BF16))
        szC2 = [szC1a, szC1a]
        y2 = esC1.enter_context(nc.sbuf_tensor("y2", [128, TAU], F32))
        y3 = esC1.enter_context(nc.sbuf_tensor("y3", [128, NJU, TAU], BF16))
        xtC1a = esC1.enter_context(nc.sbuf_tensor("xtC0", [128, TAU // 128, DM], F32))
        xtC2 = [xtC1a, xtC1a]
        s_sb = esC1.enter_context(nc.sbuf_tensor("s_sb", [128, DM], F32))
        sq_sb = esC1.enter_context(nc.sbuf_tensor("sq_sb", [128, DM], F32))
        x1_sb = esC1.enter_context(nc.sbuf_tensor("x1_sb", [128, DM], F32))
        mu = esC1.enter_context(nc.sbuf_tensor("mu", [128, 1], F32))
        s2c = esC1.enter_context(nc.sbuf_tensor("s2c", [128, 1], F32))
        var = esC1.enter_context(nc.sbuf_tensor("var", [128, 1], F32))
        rstd = esC1.enter_context(nc.sbuf_tensor("rstd", [128, 1], F32))
        rs_a = esC1.enter_context(nc.sbuf_tensor("rs_a", [128, 1], F32))
        rs_b = esC1.enter_context(nc.sbuf_tensor("rs_b", [128, 1], F32))
        pso = [esC1.enter_context(nc.psum_tensor(f"pso{i}", [128, 512], F32))
               for i in range(2)]

        for k in range(NJU):
            p.dma("sync", wstgC[:, 0:DM], d_wo[k * 128:(k + 1) * 128, :],
                  writes=["wstgC"])
            p.op("vector", lambda v, k=k: v.tensor_copy(
                out=wo_sb[:, k, :], in_=wstgC[:, 0:DM]),
                reads=["wstgC"], writes=["wo_sb"])

        NT = TLOC // TAU
        for i in range(NT):
            c0 = i * TAU
            ip = i % 2
            ys_sb, ucC, szC, xtC = ys_sb2[ip], ucC2[ip], szC2[ip], xtC2[ip]
            B_ys, B_ucC, B_szC, B_xtC = (f"ys_sb{ip}", f"ucC{ip}",
                                         "szC0", "xtC0")

            def ys_load(g, i=i, ys_sb=ys_sb):
                g.reg_mul(rt2, rdh, TLOC)
                g.reg_add(rt2, rt2, i * TAU)
                return g.dma_start(
                    out=ys_sb[:],
                    in_=bass.AP(ag_ys_out, rt2,
                                [[L, 128], [128 * L, NJU], [1, TAU]]))
            p.op("gpsimd", ys_load, reads=["ag_ys_out_d"], writes=[B_ys],
                 dma=True, sem_override=f"ry{i}")
            p.dma("sync", ucC[:], ag_uc_in.ap()
                  .rearrange("(j p) t -> p j t", p=128)[:, :, c0:c0 + TAU],
                  reads=["ag_uc_in_d"], writes=[B_ucC])
            p.dma("sync", szC[:], sz_priv.ap()
                  .rearrange("(j p) t -> p j t", p=128)[:, :, c0:c0 + TAU],
                  reads=["sz_priv_d"], writes=[B_szC])
            p.dma("sync", xtC[:], d_xtok[c0:c0 + TAU, :]
                  .rearrange("(q p) m -> p q m", p=128), writes=[B_xtC])
            for j in range(NJU):
                p.op("vector", lambda v, j=j, ucC=ucC, ys_sb=ys_sb:
                     v.scalar_tensor_tensor(
                         out=y2[:], in0=ucC[:, j, :], scalar=dp_sb[:, j:j + 1],
                         in1=ys_sb[:, j, :], op0=OP.mult, op1=OP.add),
                     reads=[B_ucC, B_ys], imm_reads=["dp_sb"],
                     writes=["y2"])
                p.op("vector", lambda v, j=j, szC=szC: v.tensor_tensor(
                    out=y3[:, j, :], in0=y2[:], in1=szC[:, j, :], op=OP.mult),
                    reads=["y2", B_szC], writes=["y3"])
            for q in range(TAU // 128):
                for half in range(2):
                    bank = pso[half]
                    bname = f"pso{half}"

                    def mm_o(t, q=q, half=half, bank=bank):
                        r = None
                        for k in range(NJU):
                            r = t.matmul(
                                bank[:, 0:512],
                                y3[:, k, q * 128:(q + 1) * 128],
                                wo_sb[:, k, half * 512:(half + 1) * 512],
                                start=(k == 0), stop=(k == NJU - 1))
                        return r
                    p.op("tensor", mm_o, reads=["y3", "wo_sb"],
                         writes=[bname])
                    p.op("vector", lambda v, q=q, half=half, bank=bank,
                         xtC=xtC: v.tensor_tensor(
                             out=s_sb[:, half * 512:(half + 1) * 512],
                             in0=bank[:, 0:512],
                             in1=xtC[:, q, half * 512:(half + 1) * 512],
                             op=OP.add),
                         reads=[bname, B_xtC], writes=["s_sb"])
                p.op("vector", lambda v: v.tensor_reduce(
                    out=mu[:], in_=s_sb[:], axis=AX.X, op=OP.add),
                    reads=["s_sb"], writes=["mu"])
                p.op("scalar", lambda s: s.activation(
                    out=sq_sb[:], in_=s_sb[:], func=AF.Square,
                    accum_out=s2c[:]), reads=["s_sb"],
                    writes=["sq_sb", "s2c"])
                p.op("vector", lambda v: v.tensor_scalar(
                    out=mu[:], in0=mu[:], scalar1=1.0 / DM, scalar2=None,
                    op0=OP.mult), reads=["mu"], writes=["mu"])
                p.op("vector", lambda v: v.tensor_scalar(
                    out=var[:], in0=s2c[:], scalar1=1.0 / DM, scalar2=None,
                    op0=OP.mult), reads=["s2c"], writes=["var"])
                p.op("vector", lambda v: v.tensor_tensor(
                    out=rs_b[:], in0=mu[:], in1=mu[:], op=OP.mult),
                    reads=["mu"], writes=["rs_b"])
                p.op("vector", lambda v: v.tensor_tensor(
                    out=var[:], in0=var[:], in1=rs_b[:], op=OP.subtract),
                    reads=["var", "rs_b"], writes=["var"])
                emit_rsqrt(p, rstd, "rs_d", var[:], "var", 1e-5, rs_a, rs_b,
                           128)
                p.op("vector", lambda v: v.tensor_scalar(
                    out=x1_sb[:], in0=s_sb[:], scalar1=mu[:, 0:1],
                    scalar2=rstd[:, 0:1], op0=OP.subtract, op1=OP.mult),
                    reads=["s_sb"], imm_reads=["mu", "rs_d"],
                    writes=["x1_sb"])
                p.dma("scalar", x1_priv.ap()[c0 + q * 128:c0 + (q + 1) * 128, :],
                      x1_sb[:], reads=["x1_sb"], writes=["x1_priv_d"])
        if DEBUG_TAPS:
            p.dma("sync", taps["t_x1"], x1_priv.ap(),
                  reads=["x1_priv_d"], writes=["tap_x1"])
        p.flush()

    if BUILD_PHASES == "ABC1":
        p.drain_all()
        p.flush()
        es.close()
        return nc

    p.fence()
    IC = [512, 512, 512, 512, 512, 256]
    with contextlib.ExitStack() as esC2:
        wut = esC2.enter_context(nc.sbuf_tensor("wut", [128, NKD, IFF], BF16))
        wdt = esC2.enter_context(nc.sbuf_tensor("wdt", [128, IFF // 128, DM], BF16))
        x1t2 = [esC2.enter_context(nc.sbuf_tensor(f"x1t{i}", [128, DM], F32))
                for i in range(2)]
        xq1 = esC2.enter_context(nc.sbuf_tensor("xq0", [128, DM], BF16))
        xq2 = [xq1, xq1]
        xqf1 = esC2.enter_context(nc.sbuf_tensor("xqf0", [128, NKD, 128], BF16))
        xqf2 = [xqf1, xqf1]
        sg1 = esC2.enter_context(nc.sbuf_tensor("sg2_0", [128, 1024], BF16))
        sg2 = [sg1, sg1]
        esC2_hfp = [esC2.enter_context(
            nc.sbuf_tensor(f"hfp{i}", [128, 1024], F32)) for i in range(2)]
        hq1 = esC2.enter_context(nc.sbuf_tensor("hq0", [128, IFF], BF16))
        hq2 = [hq1, hq1]
        hqf1 = esC2.enter_context(nc.sbuf_tensor("hqf", [128, IFF // 128, 128], BF16))
        hqf2 = [hqf1, hqf1]
        o_sb1 = esC2.enter_context(nc.sbuf_tensor("o_sb0", [128, DM], F32))
        o_sb2 = [o_sb1, o_sb1]
        qh_bm = esC2.enter_context(nc.sbuf_tensor("qhs_bm", [128, 64], F32))
        qh_sc = esC2.enter_context(nc.sbuf_tensor("qhs_sc", [128, 64], F32))
        qh_s16 = esC2.enter_context(nc.sbuf_tensor("qhs_s16", [128, 1024], F32))
        scrh2 = []
        for si in range(2):
            d = {k: esC2.enter_context(
                nc.sbuf_tensor(f"qh{si}_{k}", [128, 1024], F32))
                for k in ("vv", "av", "t1", "b2")}
            d["bm"] = qh_bm
            d["sc"] = qh_sc
            d["s16"] = qh_s16
            scrh2.append(d)
        muC = esC2.enter_context(nc.sbuf_tensor("muC", [128, 1], F32))
        s2C = esC2.enter_context(nc.sbuf_tensor("s2C", [128, 1], F32))
        varC = esC2.enter_context(nc.sbuf_tensor("varC", [128, 1], F32))
        rstdC = esC2.enter_context(nc.sbuf_tensor("rstdC", [128, 1], F32))
        rs_a2 = esC2.enter_context(nc.sbuf_tensor("rs_a2", [128, 1], F32))
        rs_b2 = esC2.enter_context(nc.sbuf_tensor("rs_b2", [128, 1], F32))
        psg2 = [esC2.enter_context(nc.psum_tensor(f"psg{i}", [128, 512], F32))
                for i in range(2)]
        psu2 = [esC2.enter_context(nc.psum_tensor(f"psup{i}", [128, 512], F32))
                for i in range(2)]
        psd = [esC2.enter_context(nc.psum_tensor(f"psd{i}", [128, 512], F32))
               for i in range(2)]
        pst = [esC2.enter_context(nc.psum_tensor(f"pst{i}", [128, 128], BF16))
               for i in range(2)]

        p.op("sync", lambda e: e.dma_start(
            out=wut.ap(), in_=wuq_out.ap(), transpose=True),
            reads=["wuq_out_d"], writes=["wut"], dma=True)
        p.op("sync", lambda e: e.dma_start(
            out=wdt.ap(), in_=wdq_out.ap(), transpose=True),
            reads=["wdq_out_d"], writes=["wdt"], dma=True)

        for tt in range(TLOC // 128):
            r0 = tt * 128
            pp = tt % 2
            x1t, xq, xqf = x1t2[pp], xq2[pp], xqf2[pp]
            sg_sb, hq, hqf, o_sb = sg2[pp], hq2[pp], hqf2[pp], o_sb2[pp]
            B_x1, B_xq, B_xqf = f"x1t{pp}", "xq0", "xqf0"
            B_sg, B_hq, B_hqf, B_o = ("sg2_0", "hq0", "hqf", "o_sb0")

            p.dma("sync", x1t[:], x1_priv.ap()[r0:r0 + 128, :],
                  reads=["x1_priv_d"], writes=[B_x1])
            emit_quant(p, x1t[:, 0:DM], xq[:, 0:DM], 128, DM,
                       scrh2[pp], B_x1, B_xq, pfx=f"qh{pp}",
                       mcols=(mcp, mcn), bs_pfx="qhs")
            for g in range(NKD):
                bank = pst[g % 2]
                bname = f"pst{g % 2}"
                p.op("tensor", lambda t, g=g, bank=bank, xq=xq: t.transpose(
                    bank[:], xq[:, g * 128:(g + 1) * 128], ident[:]),
                    reads=[B_xq, "ident"], writes=[bname])
                p.op("scalar", lambda s, g=g, bank=bank, xqf=xqf: s.copy(
                    out=xqf[:, g, :], in_=bank[:]),
                    reads=[bname], writes=[B_xqf])
            PAIRS_IC = [(0, (512, 512)), (1024, (512, 512)),
                        (2048, (512, 256))]
            hf_p2 = [esC2_hfp[0], esC2_hfp[1], esC2_hfp[0]]
            pend = []
            for pi in range(len(PAIRS_IC) + 1):
                if pi < len(PAIRS_IC):
                    poff, subws = PAIRS_IC[pi]
                    hf = hf_p2[pi]
                    B_hf = f"hfp{pi % 2}"
                    so = 0
                    for sb_i, sw in enumerate(subws):
                        bank_g, bank_u = psg2[sb_i], psu2[sb_i]
                        bn_g, bn_u = f"psg{sb_i}", f"psup{sb_i}"
                        io = poff + so

                        def mm_g(t, io=io, sw=sw, xqf=xqf, bank=bank_g):
                            r = None
                            for k in range(NKD):
                                r = t.matmul(bank[:, 0:sw], xqf[:, k, :],
                                             wgt[:, k, io:io + sw],
                                             start=(k == 0),
                                             stop=(k == NKD - 1))
                            return r
                        p.op("tensor", mm_g, reads=[B_xqf, "wgt"],
                             writes=[bn_g])

                        def mm_u(t, io=io, sw=sw, xqf=xqf, bank=bank_u):
                            r = None
                            for k in range(NKD):
                                r = t.matmul(bank[:, 0:sw], xqf[:, k, :],
                                             wut[:, k, io:io + sw],
                                             start=(k == 0),
                                             stop=(k == NKD - 1))
                            return r
                        p.op("tensor", mm_u, reads=[B_xqf, "wut"],
                             writes=[bn_u])
                        p.op("scalar", lambda s, so=so, sw=sw, bank=bank_g,
                             sg_sb=sg_sb: s.activation(
                                 out=sg_sb[:, so:so + sw], in_=bank[:, 0:sw],
                                 func=AF.Silu),
                             reads=[bn_g], writes=[B_sg])
                        p.op("vector", lambda v, so=so, sw=sw, sg_sb=sg_sb,
                             bank=bank_u, hf=hf: v.tensor_tensor(
                                 out=hf[:, so:so + sw],
                                 in0=sg_sb[:, so:so + sw],
                                 in1=bank[:, 0:sw], op=OP.mult),
                             reads=[B_sg, bn_u], writes=[B_hf])
                        so += sw
                    pend.append((pi, poff, sum(subws), hf, B_hf))
                if pi > 0:
                    qpi, qpoff, qpw, qhf, qB_hf = pend.pop(0)
                    emit_quant(p, qhf[:, 0:qpw], hq[:, qpoff:qpoff + qpw],
                               128, qpw, scrh2[qpi % 2], qB_hf, B_hq,
                               pfx=f"qh{qpi % 2}", mcols=(mcp, mcn),
                               bs_pfx="qhs")
            for g in range(IFF // 128):
                bank = pst[g % 2]
                bname = f"pst{g % 2}"
                p.op("tensor", lambda t, g=g, bank=bank, hq=hq: t.transpose(
                    bank[:], hq[:, g * 128:(g + 1) * 128], ident[:]),
                    reads=[B_hq, "ident"], writes=[bname])
                p.op("scalar", lambda s, g=g, bank=bank, hqf=hqf: s.copy(
                    out=hqf[:, g, :], in_=bank[:]),
                    reads=[bname], writes=[B_hqf])
            for half in range(2):
                bank = psd[half]
                bname = f"psd{half}"

                def mm_d(t, half=half, bank=bank, hqf=hqf):
                    r = None
                    for k in range(IFF // 128):
                        r = t.matmul(bank[:, 0:512], hqf[:, k, :],
                                     wdt[:, k, half * 512:(half + 1) * 512],
                                     start=(k == 0),
                                     stop=(k == IFF // 128 - 1))
                    return r
                p.op("tensor", mm_d, reads=[B_hqf, "wdt"], writes=[bname])
                p.op("vector", lambda v, half=half, bank=bank, o_sb=o_sb,
                     x1t=x1t: v.tensor_tensor(
                         out=o_sb[:, half * 512:(half + 1) * 512],
                         in0=bank[:, 0:512],
                         in1=x1t[:, half * 512:(half + 1) * 512], op=OP.add),
                     reads=[bname, B_x1], writes=[B_o])
            p.op("vector", lambda v, o_sb=o_sb: v.tensor_reduce(
                out=muC[:], in_=o_sb[:], axis=AX.X, op=OP.add),
                reads=[B_o], writes=["mu"])
            p.op("scalar", lambda s, o_sb=o_sb, pp=pp: s.activation(
                out=scrh2[pp]["t1"][:, 0:DM], in_=o_sb[:], func=AF.Square,
                accum_out=s2C[:]),
                reads=[B_o], writes=[f"qh{pp}_t1", "s2c"])
            p.op("vector", lambda v: v.tensor_scalar(
                out=muC[:], in0=muC[:], scalar1=1.0 / DM, scalar2=None,
                op0=OP.mult), reads=["mu"], writes=["mu"])
            p.op("vector", lambda v: v.tensor_scalar(
                out=varC[:], in0=s2C[:], scalar1=1.0 / DM, scalar2=None,
                op0=OP.mult), reads=["s2c"], writes=["var"])
            p.op("vector", lambda v: v.tensor_tensor(
                out=rs_b2[:], in0=muC[:], in1=muC[:], op=OP.mult),
                reads=["mu"], writes=["rs_b"])
            p.op("vector", lambda v: v.tensor_tensor(
                out=varC[:], in0=varC[:], in1=rs_b2[:], op=OP.subtract),
                reads=["var", "rs_b"], writes=["var"])
            emit_rsqrt(p, rstdC, "rs_d", varC[:], "var", 1e-5, rs_a2, rs_b2,
                       128)
            p.op("vector", lambda v, o_sb=o_sb: v.tensor_scalar(
                out=o_sb[:], in0=o_sb[:], scalar1=muC[:, 0:1],
                scalar2=rstdC[:, 0:1], op0=OP.subtract, op1=OP.mult),
                reads=[B_o], imm_reads=["mu", "rs_d"], writes=[B_o])
            p.dma("scalar", d_out[r0:r0 + 128, :], o_sb[:],
                  reads=[B_o], writes=["out_d"])
        p.drain_all()
        p.flush()

    es.close()
    return nc


_NC_CACHE = None
LAST_RESULTS = None


def _get_nc():
    global _NC_CACHE
    if _NC_CACHE is None:
        _NC_CACHE = build()
    return _NC_CACHE


def kernel(L0_attn_x, L0_attn_in_proj_w, L0_attn_conv_w, L0_attn_conv_b,
           L0_attn_x_proj_w, L0_attn_dt_proj_w, L0_attn_dt_proj_b,
           L0_attn_A_log, L0_attn_Dp, L0_attn_out_proj_w,
           L0_ffn_w_gate, L0_ffn_w_up, L0_ffn_w_down):
    x = np.asarray(L0_attn_x, np.float32)
    nc = _get_nc()
    shared = {
        "wi": np.ascontiguousarray(L0_attn_in_proj_w, np.float32),
        "cw": np.ascontiguousarray(
            L0_attn_conv_w[:, 0, :].reshape(NJU, 128, CONV)
            .transpose(1, 0, 2), np.float32),
        "cb": np.ascontiguousarray(
            L0_attn_conv_b.reshape(NJU, 128).T, np.float32),
        "wx": np.ascontiguousarray(L0_attn_x_proj_w, np.float32),
        "wd": np.ascontiguousarray(L0_attn_dt_proj_w, np.float32),
        "bd": np.ascontiguousarray(
            L0_attn_dt_proj_b.reshape(NJU, 128).T, np.float32),
        "dp": np.ascontiguousarray(L0_attn_Dp.reshape(NJU, 128).T, np.float32),
        "wo": np.ascontiguousarray(L0_attn_out_proj_w, np.float32),
    }
    in_maps = []
    for c in range(NCORE):
        b, h = c // 2, c % 2
        t0 = h * TLOC
        xT = x[b].T
        xfm = np.zeros((DM, TLOC + 3), np.float32)
        if t0 >= 3:
            xfm[:, 0:3] = xT[:, t0 - 3:t0]
        xfm[:, 3:] = xT[:, t0:t0 + TLOC]
        m = dict(shared)
        m["x_fm"] = np.ascontiguousarray(xfm)
        m["x_tok"] = np.ascontiguousarray(x[b, t0:t0 + TLOC])
        m["alogs"] = np.ascontiguousarray(
            L0_attn_A_log[h * (DI // 2):(h + 1) * (DI // 2)]
            .reshape(NJB, 128, NST).transpose(1, 0, 2), np.float32)
        m["dhv"] = np.array([[h]], np.int32)
        m["wgsh"] = np.ascontiguousarray(
            L0_ffn_w_gate[c * (IFF // 8):(c + 1) * (IFF // 8)], np.float32)
        m["wush"] = np.ascontiguousarray(
            L0_ffn_w_up[c * (IFF // 8):(c + 1) * (IFF // 8)], np.float32)
        m["wdsh"] = np.ascontiguousarray(
            L0_ffn_w_down[c * (DM // 8):(c + 1) * (DM // 8)], np.float32)
        in_maps.append(m)

    res = run_bass_kernel_spmd(nc, in_maps, core_ids=list(range(NCORE)))
    global LAST_RESULTS
    LAST_RESULTS = res
    out = np.empty((B, L, DM), np.float32)
    for c in range(NCORE):
        b, h = c // 2, c % 2
        out[b, h * TLOC:(h + 1) * TLOC] = res.results[c]["out"]
    return out
